# version 09
# baseline (speedup 1.0000x reference)
import sys
sys.path.insert(0, '/opt/trn_rl_repo')
import numpy as np
import ml_dtypes
from contextlib import ExitStack
import concourse.bass as bass
import concourse.mybir as mybir
from concourse.bass_utils import run_bass_kernel_spmd

F32 = mybir.dt.float32
BF16 = mybir.dt.bfloat16
AF = mybir.ActivationFunctionType
ALU = mybir.AluOpType
AX = mybir.AxisListType

B, S, D, H = 4, 2048, 2048, 16
TC = 1024
NFC = 16
RQ = float(1.0 / np.sqrt(128.0))
SHIFT = -20.0
NSLOT = 4
PREF = 3
NSUB = 4
DEBUG_TAPS = False


class Sem:
    def __init__(self, name):
        self.name, self.n, self.h = name, 0, None


class KB:
    ENGINES = ("sync", "gpsimd", "vector", "scalar", "tensor")

    def __init__(self):
        self.ops = {e: [] for e in self.ENGINES}
        self.sems = []
        self.audit = []

    def sem(self, name):
        s = Sem(name)
        self.sems.append(s)
        return s

    def emit(self, eng, fn, waits=(), inc=None, incval=1):
        waits = tuple(waits)
        for s, v in waits:
            self.audit.append((s, v, eng))

        def closure(e, fn=fn, waits=waits, inc=inc, incval=incval):
            for s, v in waits:
                if v > 0:
                    e.wait_ge(s.h, v)
            inst = fn(e)
            if inc is not None:
                if incval == "default":
                    inst.then_inc(inc.h)
                else:
                    inst.then_inc(inc.h, incval)
        self.ops[eng].append(closure)
        if inc is not None:
            inc.n += 1 if incval == "default" else incval

    def pe(self, fn, waits=(), inc=None):
        self.emit("tensor", fn, waits, inc)

    def dve(self, fn, waits=(), inc=None):
        sev = self._sev
        waits = list(waits) + [(sev, sev.n)]
        self.emit("vector", fn, waits, inc)

    def act(self, fn, waits=(), inc=None):
        sac = self._sac
        waits = list(waits) + [(sac, sac.n)]
        self.emit("scalar", fn, waits, inc)

    def io(self, fn, waits=(), inc=None):
        self.emit("sync", fn, waits, inc, incval=16)

    def gp(self, fn, waits=(), inc=None, incval=16):
        self.emit("gpsimd", fn, waits, inc, incval=incval)

    def check(self):
        for s, v, eng in self.audit:
            assert v <= s.n, f"wait on {s.name} for {v} (max {s.n}) from {eng}"


def build_program():
    nc = bass.Bass("TRN2", target_bir_lowering=False)
    kb = KB()

    x0 = nc.dram_tensor("x0", [D, TC], F32, kind="ExternalInput")
    ident = nc.dram_tensor("ident", [128, 128], F32, kind="ExternalInput")
    ropec_d = nc.dram_tensor("ropec", [128, TC], F32, kind="ExternalInput")
    ropes_d = nc.dram_tensor("ropes", [128, TC], F32, kind="ExternalInput")
    maskadd = nc.dram_tensor("maskadd", [2, 2, 8, 128, 512], BF16,
                             kind="ExternalInput")
    biasv_d = nc.dram_tensor("biasv", [128, 64], F32, kind="ExternalInput")
    wqkv_a = nc.dram_tensor("wqkv_a", [48, D, 128], F32, kind="ExternalInput")
    wout_a = nc.dram_tensor("wout_a", [16, D, 128], F32, kind="ExternalInput")
    wqkv_b = nc.dram_tensor("wqkv_b", [48, D, 128], F32, kind="ExternalInput")
    wo_b = nc.dram_tensor("wo_b", [16, D, 128], F32, kind="ExternalInput")
    wgu = nc.dram_tensor("wgu", [8, 8, 2, D, 128], F32, kind="ExternalInput")
    wdn = nc.dram_tensor("wdn", [8, 16, 1024, 128], F32, kind="ExternalInput")
    wrt = nc.dram_tensor("wrt", [D, 8], F32, kind="ExternalInput")
    qnw_d = nc.dram_tensor("qnw", [128, 16], F32, kind="ExternalInput")
    knw_d = nc.dram_tensor("knw", [128, 16], F32, kind="ExternalInput")

    y = nc.dram_tensor("y", [D, TC], F32, kind="ExternalOutput")
    if DEBUG_TAPS:
        tap_log = nc.dram_tensor("tap_log", [128, 64], F32, kind="ExternalOutput")
        tap_dw = nc.dram_tensor("tap_dw", [128, 64], F32, kind="ExternalOutput")
        tap_ht = nc.dram_tensor("tap_ht", [128, 8 * TC], BF16, kind="ExternalOutput")
        tap_q = nc.dram_tensor("tap_q", [D, TC], BF16, kind="ExternalOutput")
        tap_kv = nc.dram_tensor("tap_kv", [2 * D, TC], BF16, kind="ExternalOutput")
        tap_o = nc.dram_tensor("tap_o", [D, TC], BF16, kind="ExternalOutput")
        tap_p = nc.dram_tensor("tap_p", [16 * 128, 512], BF16, kind="ExternalOutput")

    exkv = nc.dram_tensor("exkv", [2 * D, TC], BF16, kind="Internal")
    exk = exkv.ap()[0:D]
    exv = exkv.ap()[D:2 * D]
    exn = nc.dram_tensor("exn", [1, TC], F32, kind="Internal")
    exkvg = nc.dram_tensor("exkvg", [4 * D, TC], BF16, kind="Internal")
    exng = nc.dram_tensor("exng", [2, TC], F32, kind="Internal")
    qstash = nc.dram_tensor("qstash", [D, TC], BF16, kind="Internal")

    ctx = ExitStack()
    sb = lambda n_, s_, d_: ctx.enter_context(nc.sbuf_tensor(n_, s_, d_))
    psa = lambda n_, s_, d_: ctx.enter_context(nc.psum_tensor(n_, s_, d_))

    X = sb("X", [128, NFC * TC], F32)
    XB = sb("XB", [128, NFC * TC], BF16)
    MSK = sb("MSK", [128, 8 * 512], BF16)
    BIASV = sb("BIASV", [128, 64], F32)
    KH = [sb(f"KH{i}", [128, 2 * TC], BF16) for i in range(2)]
    VH = [sb(f"VH{i}", [128, 2 * TC], BF16) for i in range(2)]
    P = [sb(f"Pt{i}", [128, 512], BF16) for i in range(3)]
    WR = [sb(f"WRs{i}", [128, NFC * 128], BF16) for i in range(NSLOT)]
    BC = sb("BC", [128, 2 * TC], F32)
    STAT = sb("STAT", [128, 3 * TC], F32)
    RT = sb("RT", [128, TC], BF16)
    RTS = sb("RTS", [128, TC], BF16)
    RSC = sb("RSC", [128, TC], BF16)
    HT = [sb("HTt0", [128, 8 * TC], BF16)] * 2
    SIL = sb("SIL", [128, TC], BF16)
    SCK = [sb(f"SCKt{i}", [128, TC], BF16) for i in range(2)]
    SCV = sb("SCV", [128, TC], BF16)
    SQB = sb("SQB", [128, TC], BF16)
    QS = [sb(f"QSt{i}", [128, 512], BF16) for i in range(2)]
    QT = [sb(f"QTt{i}", [128, 512], BF16) for i in range(2)]
    RRK = sb("RRK", [128, 16], F32)
    LOG = sb("LOG", [128, 64], F32)
    DW = sb("DW", [128, 64], F32)
    TPR = sb("TPR", [128, 32], F32)
    WRS = sb("WRS", [128, NFC * 8], BF16)
    QNW = sb("QNW", [128, 16], F32)
    KNW = sb("KNW", [128, 16], F32)
    IDN = sb("IDN", [128, 128], F32)
    IDNB = sb("IDNB", [128, 128], BF16)
    ONE = sb("ONE", [128, 1], F32)
    ONEB = sb("ONEB", [128, 1], BF16)
    ONER = sb("ONER", [128, 128], F32)
    CST = sb("CST", [128, 3], F32)

    pb = [psa(f"pb{i}", [128, 512], F32) if i != 5 else
          psa("pb5b", [128, 1024], BF16) for i in range(8)]
    pbt = pb[5]

    sIN = kb.sem("sIN")
    sST = kb.sem("sST")
    sMSK = kb.sem("sMSK")
    sRING = [kb.sem(f"sRING{i}") for i in range(NSLOT)]
    sKH = [kb.sem(f"sKH{i}") for i in range(2)]
    sVH = [kb.sem(f"sVH{i}") for i in range(2)]
    sQSm = [kb.sem(f"sQS{i}") for i in range(2)]
    sSCK = [kb.sem(f"sSCK{i}") for i in range(2)]
    sSCV = kb.sem("sSCV")
    sEXN = kb.sem("sEXN")
    sMM = kb.sem("sMM")
    sEV = kb.sem("sEV")
    kb._sev = sEV
    sAC = kb.sem("sAC")
    kb._sac = sAC
    sGP = kb.sem("sGP")
    sCC = [kb.sem(f"sCC{i}") for i in range(4)]
    sAUX = kb.sem("sAUX")

    wseq = []
    wseq += [(wqkv_a.ap()[j], 16) for j in range(48)]
    wseq += [(wout_a.ap()[j], 16) for j in range(16)]
    ffn_seq = []
    for e in range(8):
        for ic in range(8):
            ffn_seq.append((wgu.ap()[e, ic, 0], 16))
            ffn_seq.append((wgu.ap()[e, ic, 1], 16))
        for of in range(16):
            ffn_seq.append((wdn.ap()[e, of], 8))
    FB1 = len(wseq)
    wseq += ffn_seq
    QB2 = len(wseq)
    wseq += [(wqkv_b.ap()[j], 16) for j in range(48)]
    wseq += [(wo_b.ap()[j], 16) for j in range(16)]
    FB2 = len(wseq)
    wseq += ffn_seq

    ring = {"emitted": 0, "released": [None] * len(wseq)}

    def ring_ensure(i):
        while ring["emitted"] <= min(i + PREF, len(wseq) - 1):
            j = ring["emitted"]
            src, nkc = wseq[j]
            slot = WR[j % NSLOT]
            waits = []
            if j >= NSLOT:
                rel = ring["released"][j - NSLOT]
                assert rel is not None
                waits.append(rel)
            src3 = src.rearrange("(kc p) o -> p kc o", p=128)
            dst3 = slot[:, 0:nkc * 128].rearrange("p (kc o) -> p kc o", kc=nkc)
            kb.gp(lambda e, d=dst3, s=src3: e.dma_start(out=d, in_=s),
                  waits=waits, inc=sRING[j % NSLOT])
            ring["emitted"] += 1

    def ring_use(i):
        ring_ensure(i)
        nuse = i // NSLOT + 1
        return WR[i % NSLOT], (sRING[i % NSLOT], 16 * nuse)

    def ring_release(i):
        ring["released"][i] = (sMM, sMM.n)

    for fc in range(NFC):
        kb.io(lambda e, fc=fc: e.dma_start(out=X[:, fc * TC:(fc + 1) * TC],
                                           in_=x0[fc * 128:(fc + 1) * 128, :]), inc=sIN)
    kb.io(lambda e: e.dma_start(out=IDN[:, :], in_=ident[:, :]), inc=sIN)
    kb.io(lambda e: e.dma_start(out=QNW[:, :], in_=qnw_d[:, :]), inc=sIN)
    kb.io(lambda e: e.dma_start(out=KNW[:, :], in_=knw_d[:, :]), inc=sIN)
    kb.io(lambda e: e.dma_start(out=BIASV[:, :], in_=biasv_d[:, :]), inc=sIN)
    kb.gp(lambda e: e.dma_start(out=RT[:, :], in_=ropec_d[:, :]), inc=sAUX)
    kb.gp(lambda e: e.dma_start(out=RTS[:, :], in_=ropes_d[:, :]), inc=sAUX)
    kb.gp(lambda e: e.dma_start(
        out=WRS[:, :].rearrange("p (kc o) -> p kc o", kc=NFC),
        in_=wrt.ap().rearrange("(kc p) o -> p kc o", p=128)), inc=sAUX)
    kb.gp(lambda e: e.memset(ONE[:, :], 1.0), inc=sGP, incval=1)
    kb.gp(lambda e: e.memset(ONEB[:, :], 1.0), inc=sGP, incval=1)
    kb.gp(lambda e: e.memset(ONER[:, :], 1.0), inc=sGP, incval=1)
    kb.gp(lambda e: e.memset(CST[:, 0:1], SHIFT), inc=sGP, incval=1)
    kb.gp(lambda e: e.memset(CST[:, 1:2], 1e-5), inc=sGP, incval=1)
    kb.gp(lambda e: e.memset(CST[:, 2:3], 1e-6), inc=sGP, incval=1)
    pre_in = (sIN, sIN.n)
    pre_gp = (sGP, sGP.n)
    pre_aux = (sAUX, sAUX.n)
    kb.dve(lambda e: e.tensor_copy(IDNB[:, :], IDN[:, :]), waits=[pre_in, pre_gp],
           inc=sEV)

    bankfree = [(sEV, 0)] * 8
    sck_rel = [(sEV, 0), (sEV, 0)]
    kh_rel = [(sEV, 0), (sEV, 0)]
    vh_rel = [(sEV, 0), (sEV, 0)]
    qs_rel = [(sEV, 0), (sEV, 0)]
    qt_rel = [(sEV, 0), (sEV, 0)]
    ht_rel = [(sEV, 0), (sEV, 0)]
    msk_rel = [(sEV, 0)]
    scv_rel = [(sEV, 0)]

    def emit_ln(eps=1e-5):
        x_ready = (sEV, sEV.n)
        for tc in range(2):
            for fc in range(NFC):
                xs = X[:, fc * TC + tc * 512: fc * TC + tc * 512 + 512]
                w = [x_ready, pre_gp, bankfree[3 + tc]] if fc == 0 else ()
                kb.pe(lambda e, xs=xs, tc=tc, fc=fc: e.matmul(
                    pb[3 + tc][0:1, :], ONE[:, :], xs,
                    start=(fc == 0), stop=(fc == NFC - 1)),
                    waits=w, inc=sMM if fc == NFC - 1 else None)
        sq_rel = {0: (sMM, 0), 1: (sMM, 0)}
        for tc in range(2):
            for fc in range(NFC):
                xs = X[:, fc * TC + tc * 512: fc * TC + tc * 512 + 512]
                sq = SQB[:, (fc % 2) * 512:(fc % 2) * 512 + 512]
                wsq = [sq_rel[fc % 2]]
                if tc == 0 and fc == 0:
                    wsq.append(x_ready)
                kb.act(lambda e, sq=sq, xs=xs: e.activation(sq, xs, AF.Square),
                       waits=wsq, inc=sAC)
                w = [(sAC, sAC.n)]
                if fc == 0:
                    w.append(bankfree[6 + tc])
                kb.pe(lambda e, sq=sq, tc=tc, fc=fc: e.matmul(
                    pb[6 + tc][0:1, :], ONEB[:, :], sq,
                    start=(fc == 0), stop=(fc == NFC - 1)),
                    waits=w, inc=sMM)
                sq_rel[fc % 2] = (sMM, sMM.n)
        mm_done = (sMM, sMM.n)
        for tc in range(2):
            sl = slice(tc * 512, tc * 512 + 512)
            sl1 = slice(TC + tc * 512, TC + tc * 512 + 512)
            kb.dve(lambda e, sl=sl, tc=tc: e.tensor_scalar(
                STAT[0:1, sl], pb[3 + tc][0:1, :], 1.0 / D, None, ALU.mult),
                waits=[mm_done], inc=sEV)
            kb.dve(lambda e, sl1=sl1, tc=tc: e.tensor_scalar(
                STAT[0:1, sl1], pb[6 + tc][0:1, :], 1.0 / D, None, ALU.mult),
                inc=sEV)
        kb.dve(lambda e: e.engine_nop(), waits=[(sEV, sEV.n)])
        kb.dve(lambda e: e.tensor_tensor(
            STAT[0:1, 2 * TC:3 * TC], STAT[0:1, 0:TC], STAT[0:1, 0:TC],
            ALU.mult), inc=sEV)
        kb.dve(lambda e: e.tensor_tensor(
            STAT[0:1, TC:2 * TC], STAT[0:1, TC:2 * TC],
            STAT[0:1, 2 * TC:3 * TC], ALU.subtract), inc=sEV)
        for i in (3, 4, 6, 7):
            bankfree[i] = (sEV, sEV.n)
        kb.act(lambda e: e.activation(STAT[0:1, 2 * TC:3 * TC],
                                      STAT[0:1, TC:2 * TC], AF.Ln,
                                      bias=CST[0:1, 1:2]),
               waits=[(sEV, sEV.n), pre_gp], inc=sAC)
        kb.act(lambda e: e.activation(STAT[0:1, 2 * TC:3 * TC],
                                      STAT[0:1, 2 * TC:3 * TC], AF.Exp,
                                      scale=-0.5), inc=sAC)
        inv_done = (sAC, sAC.n)
        for tc in range(2):
            sl = slice(tc * 512, tc * 512 + 512)
            for (so, do) in ((0, 0), (2 * TC, TC)):
                kb.pe(lambda e, sl=sl, so=so, tc=tc: e.matmul(
                    pb[4][:, :], ONER[0:1, :],
                    STAT[0:1, so + tc * 512: so + tc * 512 + 512],
                    start=True, stop=True),
                    waits=[inv_done, bankfree[4]], inc=sMM)
                kb.dve(lambda e, sl=sl, do=do, tc=tc: e.tensor_copy(
                    BC[:, do + tc * 512: do + tc * 512 + 512], pb[4][:, :]),
                    waits=[(sMM, sMM.n)], inc=sEV)
                bankfree[4] = (sEV, sEV.n)
        for fc in range(NFC):
            xs = X[:, fc * TC:(fc + 1) * TC]
            xb = XB[:, fc * TC:(fc + 1) * TC]
            kb.dve(lambda e, xb=xb, xs=xs: e.tensor_tensor(
                xb, xs, BC[:, 0:TC], ALU.subtract), inc=sEV)
            kb.dve(lambda e, xb=xb: e.tensor_tensor(
                xb, xb, BC[:, TC:2 * TC], ALU.mult), inc=sEV)
        return (sEV, sEV.n)

    nbk = {"i": 0}

    def two_banks():
        i = nbk["i"] % 2
        nbk["i"] += 1
        return [2 * i, 2 * i + 1]

    def proj_block(block_i, ln_done, nkc=16):
        slot, wv = ring_use(block_i)
        bks = two_banks()
        out = []
        for tc in range(2):
            b = bks[tc]
            for kc in range(nkc):
                lhs = slot[:, kc * 128:(kc + 1) * 128]
                rhs = XB[:, kc * TC + tc * 512: kc * TC + tc * 512 + 512]
                w = [wv, ln_done, bankfree[b]] if kc == 0 else ()
                kb.pe(lambda e, b=b, lhs=lhs, rhs=rhs, kc=kc, nkc=nkc: e.matmul(
                    pb[b][:, :], lhs, rhs, start=(kc == 0), stop=(kc == nkc - 1)),
                    waits=w, inc=sMM if kc == nkc - 1 else None)
            out.append((b, (sMM, sMM.n)))
        ring_release(block_i)
        return out

    def rope_chunk(buf, c0, T, rt_off):
        x1, x2 = buf[0:64, c0:c0 + T], buf[64:128, c0:c0 + T]
        s1, s2 = RSC[0:64, 0:T], RSC[64:128, 0:T]
        sl = slice(rt_off, rt_off + T)
        kb.dve(lambda e: e.tensor_tensor(s1, x2, RTS[64:128, sl], ALU.mult),
               waits=[pre_aux], inc=sEV)
        kb.dve(lambda e: e.tensor_tensor(s2, x1, RTS[0:64, sl], ALU.mult), inc=sEV)
        kb.dve(lambda e: e.tensor_tensor(x1, x1, RT[0:64, sl], ALU.mult), inc=sEV)
        kb.dve(lambda e: e.tensor_tensor(x2, x2, RT[64:128, sl], ALU.mult), inc=sEV)
        kb.dve(lambda e: e.tensor_tensor(x1, x1, s1, ALU.subtract), inc=sEV)
        kb.dve(lambda e: e.tensor_tensor(x2, x2, s2, ALU.add), inc=sEV)

    def emit_qkv(layer, base, ln_done, exchange_fn=None):
        isb = (layer == 1)
        sqkv_rel = {0: (sMM, 0), 1: (sMM, 0)}
        for h in range(H):
            res = proj_block(base + h, ln_done)
            sc = SCK[h % 2]
            for tc in range(2):
                b, mk = res[tc]
                kb.dve(lambda e, sc=sc, tc=tc, b=b: e.tensor_copy(
                    sc[:, tc * 512:tc * 512 + 512], pb[b][:, :]),
                    waits=[mk, sck_rel[h % 2]] if tc == 0 else [mk], inc=sEV)
                bankfree[b] = (sEV, sEV.n)
            if isb:
                for tc in range(2):
                    sl = slice(tc * 512, tc * 512 + 512)
                    kb.act(lambda e, sc=sc, sl=sl: e.activation(
                        SQB[:, sl], sc[:, sl], AF.Square),
                        waits=[(sEV, sEV.n), sqkv_rel[tc]], inc=sAC)
                    w = [(sAC, sAC.n)]
                    if h == 0:
                        w.append(bankfree[6 + tc])
                    kb.pe(lambda e, sl=sl, h=h, tc=tc: e.matmul(
                        pb[6 + tc][0:1, :], ONEB[:, :], SQB[:, sl],
                        start=(h == 0), stop=(h == H - 1)),
                        waits=w, inc=sMM)
                    sqkv_rel[tc] = (sMM, sMM.n)
                kb.dve(lambda e, sc=sc, h=h: e.tensor_scalar(
                    sc[:, :], sc[:, :], KNW[:, h:h + 1], None, ALU.mult),
                    waits=[(sAC, sAC.n)], inc=sEV)
                rope_chunk(sc, 0, TC, 0)
            w = [(sEV, sEV.n)]
            if layer == 1 and h == 0:
                w.append((sCC[0], 1))
            kb.io(lambda e, h=h, sc=sc: e.dma_start(
                out=exk[h * 128:(h + 1) * 128, :], in_=sc[:, :]),
                waits=w, inc=sSCK[h % 2])
            sck_rel[h % 2] = (sSCK[h % 2], sSCK[h % 2].n)
        if isb:
            ss_done = (sMM, sMM.n)
            for tc in range(2):
                sl = slice(tc * 512, tc * 512 + 512)
                kb.dve(lambda e, sl=sl, tc=tc: e.tensor_scalar(
                    STAT[0:1, sl], pb[6 + tc][0:1, :], 1.0 / D, None, ALU.mult),
                    waits=[ss_done, (sEXN, sEXN.n)], inc=sEV)
                bankfree[6 + tc] = (sEV, sEV.n)
            kb.act(lambda e: e.activation(STAT[0:1, 0:TC], STAT[0:1, 0:TC], AF.Ln,
                                          bias=CST[0:1, 2:3]),
                   waits=[(sEV, sEV.n)], inc=sAC)
            kb.act(lambda e: e.activation(STAT[0:1, 0:TC], STAT[0:1, 0:TC], AF.Exp,
                                          scale=-0.5), inc=sAC)
            kb.io(lambda e: e.dma_start(out=exn[0:1, :], in_=STAT[0:1, 0:TC]),
                  waits=[(sAC, sAC.n), (sCC[0], 1)], inc=sEXN)
        for h in range(H):
            res = proj_block(base + 16 + h, ln_done)
            sc = SCK[h % 2]
            for tc in range(2):
                b, mk = res[tc]
                kb.dve(lambda e, sc=sc, tc=tc, b=b: e.tensor_copy(
                    sc[:, tc * 512:tc * 512 + 512], pb[b][:, :]),
                    waits=[mk, sck_rel[h % 2]] if tc == 0 else [mk], inc=sEV)
                bankfree[b] = (sEV, sEV.n)
            evv = (sEV, sEV.n)
            for tt in range(8):
                w = [evv] if tt == 0 else ()
                if tt == 0:
                    w = [evv, bankfree[5]]
                kb.pe(lambda e, sc=sc, tt=tt: e.transpose(
                    pbt[:, tt * 128:(tt + 1) * 128],
                    sc[:, tt * 128:(tt + 1) * 128], IDNB[:, :]),
                    waits=w, inc=sMM if tt == 7 else None)
            tr_done = (sMM, sMM.n)
            sck_rel[h % 2] = tr_done
            kb.dve(lambda e: e.tensor_copy(SCV[:, :], pbt[:, :]),
                   waits=[tr_done, scv_rel[0]], inc=sEV)
            bankfree[5] = (sEV, sEV.n)
            w = [(sEV, sEV.n)]
            if layer == 1 and h == 0:
                w.append((sCC[0], 1))
            kb.io(lambda e, h=h: e.dma_start(
                out=exv[h * 128:(h + 1) * 128, :], in_=SCV[:, :]),
                waits=w, inc=sSCV)
            scv_rel[0] = (sSCV, sSCV.n)
        if exchange_fn is not None:
            exchange_fn()
        for h in range(H):
            res = proj_block(base + 32 + h, ln_done)
            sc = SCK[h % 2]
            for tc in range(2):
                b, mk = res[tc]
                kb.dve(lambda e, sc=sc, tc=tc, b=b: e.tensor_copy(
                    sc[:, tc * 512:tc * 512 + 512], pb[b][:, :]),
                    waits=[mk, sck_rel[h % 2]] if tc == 0 else [mk], inc=sEV)
                bankfree[b] = (sEV, sEV.n)
            if isb:
                for tc in range(2):
                    sl = slice(tc * 512, tc * 512 + 512)
                    kb.act(lambda e, sc=sc, sl=sl: e.activation(
                        SQB[:, sl], sc[:, sl], AF.Square),
                        waits=[(sEV, sEV.n), sqkv_rel[tc]], inc=sAC)
                    w = [(sAC, sAC.n)]
                    if h == 0:
                        w.append(bankfree[6 + tc])
                    kb.pe(lambda e, sl=sl, h=h, tc=tc: e.matmul(
                        pb[6 + tc][0:1, :], ONEB[:, :], SQB[:, sl],
                        start=(h == 0), stop=(h == H - 1)),
                        waits=w, inc=sMM)
                    sqkv_rel[tc] = (sMM, sMM.n)
            w = [(sEV, sEV.n)]
            if layer == 1 and h == 0:
                w.append((sQSm[0], sQSm[0].n))
                w.append((sQSm[1], sQSm[1].n))
            kb.io(lambda e, h=h, sc=sc: e.dma_start(
                out=qstash[h * 128:(h + 1) * 128, :], in_=sc[:, :]),
                waits=w, inc=sSCK[h % 2])
            sck_rel[h % 2] = (sSCK[h % 2], sSCK[h % 2].n)
        if isb:
            ss_done = (sMM, sMM.n)
            for tc in range(2):
                sl1 = slice(TC + tc * 512, TC + tc * 512 + 512)
                kb.dve(lambda e, sl1=sl1, tc=tc: e.tensor_scalar(
                    STAT[0:1, sl1], pb[6 + tc][0:1, :], 1.0 / D, None, ALU.mult),
                    waits=[ss_done], inc=sEV)
                bankfree[6 + tc] = (sEV, sEV.n)
            kb.act(lambda e: e.activation(STAT[0:1, TC:2 * TC],
                                          STAT[0:1, TC:2 * TC],
                                          AF.Ln, bias=CST[0:1, 2:3]),
                   waits=[(sEV, sEV.n)], inc=sAC)
            kb.act(lambda e: e.activation(STAT[0:1, TC:2 * TC],
                                          STAT[0:1, TC:2 * TC],
                                          AF.Exp, scale=-0.5), inc=sAC)
            kb.dve(lambda e: e.tensor_scalar(STAT[0:1, TC:2 * TC],
                                             STAT[0:1, TC:2 * TC],
                                             RQ, None, ALU.mult),
                   waits=[(sAC, sAC.n)], inc=sEV)
            for tc in range(2):
                sl = slice(tc * 512, tc * 512 + 512)
                kb.pe(lambda e, tc=tc: e.matmul(
                    pb[4][:, :], ONER[0:1, :],
                    STAT[0:1, TC + tc * 512: TC + tc * 512 + 512],
                    start=True, stop=True),
                    waits=[(sEV, sEV.n), bankfree[4]], inc=sMM)
                kb.dve(lambda e, sl=sl: e.tensor_copy(BC[:, sl], pb[4][:, :]),
                       waits=[(sMM, sMM.n)], inc=sEV)
                bankfree[4] = (sEV, sEV.n)
        return [(sSCK[0], sSCK[0].n), (sSCK[1], sSCK[1].n)]

    def emit_exchange(idx, with_n=False):
        waits = [(sSCK[0], sSCK[0].n), (sSCK[1], sSCK[1].n), (sSCV, sSCV.n)]
        if idx >= 1:
            waits += [(sKH[0], sKH[0].n), (sKH[1], sKH[1].n),
                      (sVH[0], sVH[0].n), (sVH[1], sVH[1].n),
                      (sIN, sIN.n)]
        kb.gp(lambda e: e.collective_compute(
            "AllGather", ALU.bypass,
            replica_groups=[[0, 1], [2, 3], [4, 5], [6, 7]],
            ins=[exkv.ap().opt()],
            outs=[exkvg.ap().opt()]),
            waits=waits, inc=sCC[idx], incval="default")
        if with_n:
            kb.gp(lambda e: e.collective_compute(
                "AllGather", ALU.bypass,
                replica_groups=[[0, 1], [2, 3], [4, 5], [6, 7]],
                ins=[exn.ap().opt()],
                outs=[exng.ap().opt()]),
                waits=[(sEXN, sEXN.n)], inc=sCC[idx + 1], incval="default")

    def emit_attn(layer, maskt, cc_idx, stash_done):
        isb = (layer == 1)
        ccw = (sCC[cc_idx], 1)
        if isb:
            kb.io(lambda e: e.dma_start(out=STAT[0:1, 0:TC], in_=exng[0:1, :]),
                  waits=[ccw, (sCC[cc_idx + 1], 1), (sEXN, sEXN.n),
                         (sMM, sMM.n), (sEV, sEV.n), (sAC, sAC.n)], inc=sIN)
            kb.io(lambda e: e.dma_start(out=STAT[0:1, TC:2 * TC],
                                        in_=exng[1:2, :]), inc=sIN)
            rr_in = (sIN, sIN.n)
            for kt in range(16):
                w = [rr_in, pre_in]
                if kt == 0:
                    w.append(bankfree[4])
                src = STAT[0:1, kt * 128:(kt + 1) * 128]
                kb.pe(lambda e, kt=kt, src=src: e.transpose(
                    pb[4][:, kt:kt + 1], src, IDN[0:1, 0:1]),
                    waits=w, inc=sMM if kt == 15 else None)
            kb.dve(lambda e: e.tensor_copy(RRK[:, :], pb[4][:, 0:16]),
                   waits=[(sMM, sMM.n)], inc=sEV)
            bankfree[4] = (sEV, sEV.n)

        def reload(qc, h):
            hh = h % 2
            ksrc = exkvg.ap().rearrange("(r x) t -> x r t", r=2)[
                h * 128:(h + 1) * 128]
            kdst = KH[hh][:, :].rearrange("p (r t) -> p r t", r=2)
            kb.io(lambda e, kdst=kdst, ksrc=ksrc: e.dma_start(out=kdst, in_=ksrc),
                  waits=[ccw, kh_rel[hh]], inc=sKH[hh])
            vsrc = exkvg.ap().rearrange("(r x) t -> x r t", r=2)[
                D + h * 128:D + (h + 1) * 128]
            vdst = VH[hh][:, :].rearrange("p (r t) -> p r t", r=2)
            kb.io(lambda e, vdst=vdst, vsrc=vsrc: e.dma_start(out=vdst, in_=vsrc),
                  waits=[ccw, vh_rel[hh]], inc=sVH[hh])
            kb.io(lambda e, h=h, qc=qc, hh=hh: e.dma_start(
                out=QS[hh][:, :],
                in_=qstash[h * 128:(h + 1) * 128, qc * 512:qc * 512 + 512]),
                waits=[qs_rel[hh]] + stash_done, inc=sQSm[hh])
            return (sKH[hh], sKH[hh].n), (sVH[hh], sVH[hh].n), (sQSm[hh], sQSm[hh].n)

        exp_free = [bankfree[0], bankfree[1], bankfree[2]]
        p_extra = {}
        pend = {}
        pending_norm = [None]

        def bdset(qc):
            return list(range(4 * qc, 4 * qc + 4)) + \
                list(range(4 * qc + 8, 4 * qc + 12))

        for qc in range(2):
            ktmax = 12 if qc == 0 else 16
            bset = bdset(qc)
            bidx = {kt: j for j, kt in enumerate(bset)}
            mw = [msk_rel[0]]
            for j in range(8):
                kb.io(lambda e, j=j, qc=qc: e.dma_start(
                    out=MSK[:, j * 512:(j + 1) * 512],
                    in_=maskadd.ap()[layer, qc, j]),
                    waits=mw if j == 0 else (), inc=sMSK)
                mw = ()
            msk_w = (sMSK, sMSK.n)
            if qc == 0:
                pend[0] = reload(0, 0)
            for h in range(H):
                hh = h % 2
                khw, vhw, qsw = pend[h]
                if h + 1 < H:
                    pend[h + 1] = reload(qc, h + 1)
                elif qc == 0:
                    pend[0] = reload(1, 0)
                    pend.pop(1, None)
                if isb:
                    kb.dve(lambda e, hh=hh, qc=qc: e.tensor_tensor(
                        QT[hh][:, :], QS[hh][:, :],
                        BC[:, qc * 512:qc * 512 + 512], ALU.mult),
                        waits=[qsw, qt_rel[hh]], inc=sEV)
                    kb.dve(lambda e, hh=hh, h=h: e.tensor_scalar(
                        QT[hh][:, :], QT[hh][:, :], QNW[:, h:h + 1], None,
                        ALU.mult), inc=sEV)
                    rope_chunk(QT[hh], 0, 512, qc * 512)
                    qs_rel[hh] = (sEV, sEV.n)
                    qrhs = QT[hh]
                else:
                    qrhs = QS[hh]
                avb = 3 if h % 2 == 0 else 4
                smb = 6 if h % 2 == 0 else 7
                for kt in range(ktmax):
                    sbk = kt % 3
                    w = [exp_free[sbk]]
                    if kt < 3:
                        w.append(khw)
                        if not isb:
                            w.append(qsw)
                        else:
                            w.append((sEV, sEV.n))
                    isbd = kt in bidx
                    kb.pe(lambda e, sbk=sbk, kt=kt, qrhs=qrhs, hh=hh,
                          isbd=isbd: e.matmul(
                              pb[sbk][:, :], KH[hh][:, kt * 128:(kt + 1) * 128],
                              qrhs[:, :], start=True, stop=not isbd),
                          waits=w, inc=None if isbd else sMM)
                    if isbd:
                        j = bidx[kt]
                        kb.pe(lambda e, sbk=sbk, j=j: e.matmul(
                            pb[sbk][:, :], IDNB[:, :],
                            MSK[:, j * 512:(j + 1) * 512],
                            start=False, stop=True),
                            waits=[msk_w], inc=sMM)
                    scl = RRK[:, kt:kt + 1] if isb else RQ
                    bia = BIASV[:, layer * 32 + qc * 16 + kt:
                                layer * 32 + qc * 16 + kt + 1]
                    wexp = [(sMM, sMM.n), p_free[sbk]]
                    kb.act(lambda e, sbk=sbk, scl=scl, bia=bia: e.activation(
                        P[sbk][:, :], pb[sbk][:, :], AF.Exp,
                        bias=bia, scale=scl),
                        waits=wexp, inc=sAC)
                    exp_free[sbk] = (sAC, sAC.n)
                    pmask = (sAC, sAC.n)
                    w2 = [pmask]
                    if kt == 0:
                        w2 += [vhw, bankfree[avb], bankfree[smb]]
                    kb.pe(lambda e, sbk=sbk, kt=kt, hh=hh, avb=avb,
                          ktmax=ktmax: e.matmul(
                              pb[avb][:, :], VH[hh][:, kt * 128:(kt + 1) * 128],
                              P[sbk][:, :], start=(kt == 0),
                              stop=(kt == ktmax - 1)),
                          waits=w2)
                    kb.pe(lambda e, sbk=sbk, kt=kt, smb=smb, ktmax=ktmax: e.matmul(
                        pb[smb][0:1, :], ONEB[:, :], P[sbk][:, :],
                        start=(kt == 0), stop=(kt == ktmax - 1)),
                        inc=sMM)
                    p_free[sbk] = (sMM, sMM.n)
                av_done = (sMM, sMM.n)
                kh_rel[hh] = av_done
                vh_rel[hh] = av_done
                if not isb:
                    qs_rel[hh] = av_done
                else:
                    qt_rel[hh] = av_done
                if pending_norm[0] is not None:
                    pending_norm[0]()

                def mk_norm(h=h, qc=qc, avb=avb, smb=smb, av_done=av_done):
                    def norm():
                        kb.dve(lambda e: e.reciprocal(
                            STAT[0:1, 2 * TC:2 * TC + 512], pb[smb][0:1, :]),
                            waits=[av_done], inc=sEV)
                        bankfree[smb] = (sEV, sEV.n)
                        kb.pe(lambda e: e.matmul(
                            pb[smb][:, :], ONER[0:1, :],
                            STAT[0:1, 2 * TC:2 * TC + 512],
                            start=True, stop=True),
                            waits=[(sEV, sEV.n)], inc=sMM)
                        kb.dve(lambda e: e.tensor_copy(
                            BC[:, 3 * 512:4 * 512], pb[smb][:, :]),
                            waits=[(sMM, sMM.n)], inc=sEV)
                        bankfree[smb] = (sEV, sEV.n)
                        kb.dve(lambda e: e.tensor_tensor(
                            XB[:, h * TC + qc * 512: h * TC + qc * 512 + 512],
                            pb[avb][:, :], BC[:, 3 * 512:4 * 512], ALU.mult),
                            waits=[av_done], inc=sEV)
                        bankfree[avb] = (sEV, sEV.n)
                    return norm
                pending_norm[0] = mk_norm()
            msk_rel[0] = (sMM, sMM.n)
        pending_norm[0]()
        for i in range(3):
            bankfree[i] = exp_free[i]

    def emit_outproj(base):
        o_ready = (sEV, sEV.n)
        for of in range(NFC):
            res = proj_block(base + of, o_ready)
            for tc in range(2):
                b, mk = res[tc]
                xsl = X[:, of * TC + tc * 512: of * TC + tc * 512 + 512]
                kb.dve(lambda e, xsl=xsl, b=b: e.tensor_tensor(
                    xsl, pb[b][:, :], xsl, ALU.add), waits=[mk], inc=sEV)
                bankfree[b] = (sEV, sEV.n)

    def emit_ffn(base, ln_done):
        sil_rel = [(sEV, 0), (sEV, 0)]
        for tt in range(8):
            b = 3 + tt % 2
            for fc in range(NFC):
                w = []
                if fc == 0:
                    w = [ln_done, pre_aux, bankfree[b]]
                kb.pe(lambda e, b=b, fc=fc, tt=tt: e.matmul(
                    pb[b][0:128, 0:8],
                    XB[:, fc * TC + tt * 128: fc * TC + tt * 128 + 128],
                    WRS[:, fc * 8:(fc + 1) * 8],
                    start=(fc == 0), stop=(fc == NFC - 1)),
                    waits=w, inc=sMM if fc == NFC - 1 else None)
            kb.dve(lambda e, b=b, tt=tt: e.tensor_copy(
                LOG[:, tt * 8:(tt + 1) * 8], pb[b][0:128, 0:8]),
                waits=[(sMM, sMM.n)], inc=sEV)
            bankfree[b] = (sEV, sEV.n)
        def dves(fn):
            kb.dve(fn, waits=[(sEV, sEV.n)], inc=sEV)
        for tt in range(8):
            lg = LOG[:, tt * 8:(tt + 1) * 8]
            dves(lambda e, lg=lg: e.reduce_max(TPR[:, 0:1], lg, axis=AX.X))
            dves(lambda e, lg=lg: e.tensor_scalar(
                TPR[:, 4:12], lg, TPR[:, 0:1], None, ALU.is_equal))
            dves(lambda e: e.tensor_scalar(
                TPR[:, 20:28], TPR[:, 4:12], -1e30, None, ALU.mult))
            dves(lambda e, lg=lg: e.tensor_tensor(
                TPR[:, 20:28], TPR[:, 20:28], lg, ALU.add))
            dves(lambda e: e.reduce_max(TPR[:, 1:2], TPR[:, 20:28], axis=AX.X))
            dves(lambda e, lg=lg: e.tensor_scalar(
                TPR[:, 12:20], lg, TPR[:, 1:2], None, ALU.is_equal))
            dves(lambda e: e.tensor_tensor(
                TPR[:, 2:3], TPR[:, 1:2], TPR[:, 0:1], ALU.subtract))
            kb.act(lambda e: e.activation(TPR[:, 2:3], TPR[:, 2:3], AF.Exp),
                   waits=[(sEV, sEV.n)], inc=sAC)
            kb.dve(lambda e: e.tensor_scalar(
                TPR[:, 28:29], TPR[:, 2:3], 1.0, None, ALU.add),
                waits=[(sAC, sAC.n)], inc=sEV)
            dves(lambda e: e.reciprocal(TPR[:, 2:3], TPR[:, 28:29]))
            dves(lambda e: e.tensor_scalar(
                TPR[:, 3:4], TPR[:, 2:3], -1.0, 1.0, ALU.mult, ALU.add))
            dves(lambda e: e.tensor_scalar(
                TPR[:, 4:12], TPR[:, 4:12], TPR[:, 2:3], None, ALU.mult))
            dves(lambda e: e.tensor_scalar(
                TPR[:, 12:20], TPR[:, 12:20], TPR[:, 3:4], None, ALU.mult))
            dves(lambda e, tt=tt: e.tensor_tensor(
                DW[:, tt * 8:(tt + 1) * 8], TPR[:, 4:12], TPR[:, 12:20], ALU.add))
        dw_done = (sEV, sEV.n)
        if DEBUG_TAPS:
            kb.io(lambda e: e.dma_start(out=tap_log.ap(), in_=LOG[:, :]),
                  waits=[dw_done], inc=sST)
            kb.io(lambda e: e.dma_start(out=tap_dw.ap(), in_=DW[:, :]),
                  waits=[dw_done], inc=sST)
        for e_ in range(8):
            for tc in range(2):
                for ttl in range(4):
                    tt = tc * 4 + ttl
                    w = [dw_done, pre_in]
                    if ttl == 0:
                        w.append(bankfree[7])
                    kb.pe(lambda e, tt=tt, ttl=ttl, e_=e_: e.transpose(
                        pb[7][0:1, ttl * 128:(ttl + 1) * 128],
                        DW[:, tt * 8 + e_: tt * 8 + e_ + 1], IDN[:, :]),
                        waits=w, inc=sMM if ttl == 3 else None)
                kb.dve(lambda e, tc=tc: e.tensor_copy(
                    STAT[0:1, 2 * TC + tc * 512:2 * TC + tc * 512 + 512],
                    pb[7][0:1, 0:512]),
                    waits=[(sMM, sMM.n)], inc=sEV)
                bankfree[7] = (sEV, sEV.n)
                kb.pe(lambda e, tc=tc: e.matmul(
                    pb[6][:, :], ONER[0:1, :],
                    STAT[0:1, 2 * TC + tc * 512:2 * TC + tc * 512 + 512],
                    start=True, stop=True),
                    waits=[(sEV, sEV.n), bankfree[6]], inc=sMM)
                kb.dve(lambda e, tc=tc: e.tensor_copy(
                    SQB[:, tc * 512:tc * 512 + 512], pb[6][:, :]),
                    waits=[(sMM, sMM.n)], inc=sEV)
                bankfree[6] = (sEV, sEV.n)
            bw_done = (sEV, sEV.n)
            for ic in range(8):
                gi = base + e_ * 32 + 2 * ic
                slotg, wg = ring_use(gi)
                for tc in range(2):
                    for kc in range(NFC):
                        w = [wg, ln_done, bankfree[tc]] if kc == 0 else ()
                        kb.pe(lambda e, tc=tc, kc=kc, slotg=slotg: e.matmul(
                            pb[tc][:, :], slotg[:, kc * 128:(kc + 1) * 128],
                            XB[:, kc * TC + tc * 512: kc * TC + tc * 512 + 512],
                            start=(kc == 0), stop=(kc == NFC - 1)),
                            waits=w, inc=sMM if kc == NFC - 1 else None)
                    kb.act(lambda e, tc=tc: e.activation(
                        SIL[:, tc * 512:tc * 512 + 512], pb[tc][:, :], AF.Silu),
                        waits=[(sMM, sMM.n), sil_rel[tc]], inc=sAC)
                    bankfree[tc] = (sAC, sAC.n)
                ring_release(gi)
                ui = gi + 1
                slotu, wu = ring_use(ui)
                for tc in range(2):
                    for kc in range(NFC):
                        w = [wu, bankfree[2 + tc]] if kc == 0 else ()
                        kb.pe(lambda e, tc=tc, kc=kc, slotu=slotu: e.matmul(
                            pb[2 + tc][:, :], slotu[:, kc * 128:(kc + 1) * 128],
                            XB[:, kc * TC + tc * 512: kc * TC + tc * 512 + 512],
                            start=(kc == 0), stop=(kc == NFC - 1)),
                            waits=w, inc=sMM if kc == NFC - 1 else None)
                    hsl = HT[e_ % 2][:, ic * TC + tc * 512: ic * TC + tc * 512 + 512]
                    w = [(sMM, sMM.n), (sAC, sAC.n)]
                    if ic == 0 and tc == 0:
                        w.append(ht_rel[0])
                        w.append(ht_rel[1])
                    kb.dve(lambda e, hsl=hsl, tc=tc: e.tensor_tensor(
                        hsl, pb[2 + tc][:, :], SIL[:, tc * 512:tc * 512 + 512],
                        ALU.mult), waits=w, inc=sEV)
                    sil_rel[tc] = (sEV, sEV.n)
                    kb.dve(lambda e, hsl=hsl, tc=tc: e.tensor_tensor(
                        hsl, hsl, SQB[:, tc * 512:tc * 512 + 512], ALU.mult),
                        waits=[bw_done], inc=sEV)
                    bankfree[2 + tc] = (sEV, sEV.n)
                ring_release(ui)
            ht_done = (sEV, sEV.n)
            if DEBUG_TAPS and e_ == 0 and base == FB1:
                kb.io(lambda e: e.dma_start(out=tap_ht.ap(), in_=HT[0][:, :]),
                      waits=[ht_done], inc=sST)
            bankfree[5] = ht_done
            for of in range(NFC):
                di = base + e_ * 32 + 16 + of
                slotd, wd = ring_use(di)
                for tc in range(2):
                    bd = 4 if (of * 2 + tc) % 2 == 0 else 7
                    for kc in range(8):
                        w = [wd, ht_done, bankfree[bd]] if kc == 0 else ()
                        kb.pe(lambda e, bd=bd, kc=kc, tc=tc, slotd=slotd,
                              e_=e_: e.matmul(
                                  pb[bd][:, :], slotd[:, kc * 128:(kc + 1) * 128],
                                  HT[e_ % 2][:, kc * TC + tc * 512:
                                             kc * TC + tc * 512 + 512],
                                  start=(kc == 0), stop=(kc == 7)),
                              waits=w, inc=sMM if kc == 7 else None)
                    xsl = X[:, of * TC + tc * 512: of * TC + tc * 512 + 512]
                    kb.dve(lambda e, xsl=xsl, bd=bd: e.tensor_tensor(
                        xsl, pb[bd][:, :], xsl, ALU.add),
                        waits=[(sMM, sMM.n)], inc=sEV)
                    bankfree[bd] = (sEV, sEV.n)
                ring_release(di)
            ht_rel[0] = ht_rel[1] = (sMM, sMM.n)

    p_free = [(sMM, 0), (sMM, 0), (sMM, 0)]

    ln1 = emit_ln()
    stash1 = emit_qkv(0, 0, ln1, exchange_fn=lambda: emit_exchange(0))
    if NSUB >= 1:
        emit_attn(0, None, 0, stash1)
        if DEBUG_TAPS and NSUB == 1:
            ev = (sEV, sEV.n)
            for fc in range(NFC):
                kb.io(lambda e, fc=fc: e.dma_start(
                    out=tap_o[fc * 128:(fc + 1) * 128, :],
                    in_=XB[:, fc * TC:(fc + 1) * TC]),
                    waits=[ev] if fc == 0 else (), inc=sST)
            kb.io(lambda e: e.dma_start(out=tap_q.ap(), in_=qstash.ap()), inc=sST)
            kb.io(lambda e: e.dma_start(out=tap_kv.ap(), in_=exkv.ap()), inc=sST)
        emit_outproj(48)
    if NSUB >= 2:
        ln2 = emit_ln()
        if DEBUG_TAPS:
            for fc in range(NFC):
                kb.io(lambda e, fc=fc: e.dma_start(
                    out=tap_o[fc * 128:(fc + 1) * 128, :],
                    in_=XB[:, fc * TC:(fc + 1) * TC]),
                    waits=[ln2] if fc == 0 else (), inc=sST)
        emit_ffn(FB1, ln2)
    if NSUB >= 3:
        ln3 = emit_ln()
        stash2 = emit_qkv(1, QB2, ln3,
                          exchange_fn=lambda: emit_exchange(1, with_n=True))
        emit_attn(1, None, 1, stash2)
        emit_outproj(QB2 + 48)
    if NSUB >= 4:
        ln4 = emit_ln()
        emit_ffn(FB2, ln4)

    fin = (sEV, sEV.n)
    for fc in range(NFC):
        kb.io(lambda e, fc=fc: e.dma_start(out=y[fc * 128:(fc + 1) * 128, :],
                                           in_=X[:, fc * TC:(fc + 1) * TC]),
              waits=[fin] if fc == 0 else (), inc=sST)
    kb.emit("sync", lambda e: None, waits=[(sST, sST.n)])

    kb.check()

    for s in kb.sems:
        s.h = ctx.enter_context(nc.semaphore(s.name))
    with nc.Block() as block:
        for eng in KB.ENGINES:
            getattr(block, eng)(
                (lambda ops: lambda e: [f(e) for f in ops])(kb.ops[eng]))
    ctx.close()
    return nc


def _host_prepare(inputs):
    f32 = np.float32
    bf = ml_dtypes.bfloat16
    wqa = np.asarray(inputs["L0_attn_a_w_qkv"], f32)
    woa = np.asarray(inputs["L0_attn_a_w_out"], f32)
    wqb = np.asarray(inputs["L0_attn_b_w_qkv"], f32).T
    wob = np.asarray(inputs["L0_attn_b_w_o"], f32).T
    gu = np.asarray(inputs["L0_ffn_expert_gate_up"], f32)
    dn = np.asarray(inputs["L0_ffn_expert_down"], f32)
    rg = np.asarray(inputs["L0_ffn_router_gate"], f32)
    qnw = np.asarray(inputs["L0_attn_b_q_norm_w"], f32)
    knw = np.asarray(inputs["L0_attn_b_k_norm_w"], f32)

    def qkv_blocks(w):
        cols = list(range(16, 32)) + list(range(32, 48)) + list(range(0, 16))
        return np.ascontiguousarray(
            np.stack([w[:, j * 128:(j + 1) * 128] for j in cols], axis=0))

    def of_blocks(w, n):
        return np.ascontiguousarray(
            np.stack([w[:, j * 128:(j + 1) * 128] for j in range(n)], axis=0))

    shared = dict(
        wqkv_a=qkv_blocks(wqa),
        wout_a=of_blocks(woa, 16),
        wqkv_b=qkv_blocks(wqb),
        wo_b=of_blocks(wob, 16),
        wgu=np.ascontiguousarray(
            np.stack([np.stack([np.stack([
                gu[e][:, 0::2][:, ic * 128:(ic + 1) * 128],
                gu[e][:, 1::2][:, ic * 128:(ic + 1) * 128]], 0)
                for ic in range(8)], 0) for e in range(8)], 0)),
        wdn=np.ascontiguousarray(
            np.stack([np.stack([dn[e][:, j * 128:(j + 1) * 128]
                                for j in range(16)], 0) for e in range(8)], 0)),
        wrt=np.ascontiguousarray(rg.T),
        qnw=np.ascontiguousarray(qnw.reshape(16, 128).T),
        knw=np.ascontiguousarray(knw.reshape(16, 128).T),
        ident=np.eye(128, dtype=f32),
    )

    x = np.asarray(inputs["L0_attn_a_x"], f32)
    inv_freq = 1.0 / (10000.0 ** (np.arange(0, 128, 2) / 128.0))

    per_core = []
    for c in range(8):
        b, p = c // 2, c % 2
        pos = np.arange(p * 1024, (p + 1) * 1024, dtype=np.float64)
        fr = pos[None, :] * inv_freq[:, None]
        ropec = np.concatenate([np.cos(fr), np.cos(fr)], 0).astype(f32)
        ropes = np.concatenate([np.sin(fr), np.sin(fr)], 0).astype(f32)
        qa = p * 1024 + np.arange(1024)
        ka = np.arange(2048)
        allow_a = (ka[:, None] <= qa[None, :])
        allow_b = allow_a & (qa[None, :] - ka[:, None] <= 1023)
        KILL = np.float32(-1e6)
        SHIFT_ = np.float32(-20.0)
        maskadd = np.zeros((2, 2, 8, 128, 512), f32)
        biasv = np.zeros((128, 64), f32)
        for li, allow in ((0, allow_a), (1, allow_b)):
            for qc in range(2):
                bset = list(range(4 * qc, 4 * qc + 4)) + \
                    list(range(4 * qc + 8, 4 * qc + 12))
                qsl = slice(qc * 512, qc * 512 + 512)
                for j, kt in enumerate(bset):
                    al = allow[kt * 128:(kt + 1) * 128, qsl]
                    maskadd[li, qc, j] = np.where(al, 0.0, KILL)
                for kt in range(16):
                    col = li * 32 + qc * 16 + kt
                    if kt in bset:
                        biasv[:, col] = SHIFT_
                    else:
                        al = allow[kt * 128:(kt + 1) * 128, qsl]
                        full = al.all()
                        none = not al.any()
                        assert full or none, (li, qc, kt, p)
                        biasv[:, col] = SHIFT_ if full else (SHIFT_ + KILL)
        d = dict(shared)
        d["x0"] = np.ascontiguousarray(x[b, p * 1024:(p + 1) * 1024, :].T)
        d["ropec"] = ropec
        d["ropes"] = ropes
        d["maskadd"] = maskadd.astype(bf)
        d["biasv"] = biasv
        per_core.append(d)
    return per_core


def kernel(**inputs):
    per_core = _host_prepare(inputs)
    nc = build_program()
    res = run_bass_kernel_spmd(nc, per_core, core_ids=list(range(8)))
    out = np.empty((B, S, D), np.float32)
    for c in range(8):
        b, p = c // 2, c % 2
        out[b, p * 1024:(p + 1) * 1024, :] = res.results[c]["y"].T
    return out


if __name__ == "__main__":
    nc = build_program()
    print("n_inst:", nc.n_instructions())


# version 10
# speedup vs baseline: 1.5348x; 1.5348x over previous
import sys
sys.path.insert(0, '/opt/trn_rl_repo')
import numpy as np
import ml_dtypes
from contextlib import ExitStack
import concourse.bass as bass
import concourse.mybir as mybir
from concourse.bass_utils import run_bass_kernel_spmd

F32 = mybir.dt.float32
BF16 = mybir.dt.bfloat16
AF = mybir.ActivationFunctionType
ALU = mybir.AluOpType
AX = mybir.AxisListType

B, S, D, H = 4, 2048, 2048, 16
TC = 1024
NFC = 16
RQ = float(1.0 / np.sqrt(128.0))
SHIFT = -20.0
NSLOT = 4
PREF = 3
NSUB = 4
DEBUG_TAPS = False


class Sem:
    def __init__(self, name):
        self.name, self.n, self.h = name, 0, None


class KB:
    ENGINES = ("sync", "gpsimd", "vector", "scalar", "tensor")

    def __init__(self):
        self.ops = {e: [] for e in self.ENGINES}
        self.sems = []
        self.audit = []

    def sem(self, name):
        s = Sem(name)
        self.sems.append(s)
        return s

    def emit(self, eng, fn, waits=(), inc=None, incval=1):
        waits = tuple(waits)
        for s, v in waits:
            self.audit.append((s, v, eng))

        def closure(e, fn=fn, waits=waits, inc=inc, incval=incval):
            for s, v in waits:
                if v > 0:
                    e.wait_ge(s.h, v)
            inst = fn(e)
            if inc is not None:
                if incval == "default":
                    inst.then_inc(inc.h)
                else:
                    inst.then_inc(inc.h, incval)
        self.ops[eng].append(closure)
        if inc is not None:
            inc.n += 1 if incval == "default" else incval

    def pe(self, fn, waits=(), inc=None):
        self.emit("tensor", fn, waits, inc)

    def dve(self, fn, waits=(), inc=None):
        sev = self._sev
        waits = list(waits) + [(sev, sev.n)]
        self.emit("vector", fn, waits, inc)

    def act(self, fn, waits=(), inc=None):
        sac = self._sac
        waits = list(waits) + [(sac, sac.n)]
        self.emit("scalar", fn, waits, inc)

    def io(self, fn, waits=(), inc=None):
        self.emit("sync", fn, waits, inc, incval=16)

    def gp(self, fn, waits=(), inc=None, incval=16):
        self.emit("gpsimd", fn, waits, inc, incval=incval)

    def check(self):
        for s, v, eng in self.audit:
            assert v <= s.n, f"wait on {s.name} for {v} (max {s.n}) from {eng}"


def build_program():
    nc = bass.Bass("TRN2", target_bir_lowering=False)
    kb = KB()

    x0 = nc.dram_tensor("x0", [D, TC], F32, kind="ExternalInput")
    ident = nc.dram_tensor("ident", [128, 128], F32, kind="ExternalInput")
    ropec_d = nc.dram_tensor("ropec", [128, TC], F32, kind="ExternalInput")
    ropes_d = nc.dram_tensor("ropes", [128, TC], F32, kind="ExternalInput")
    maskadd = nc.dram_tensor("maskadd", [2, 2, 8, 128, 512], BF16,
                             kind="ExternalInput")
    biasv_d = nc.dram_tensor("biasv", [128, 64], F32, kind="ExternalInput")
    wqkv_a = nc.dram_tensor("wqkv_a", [48, D, 128], F32, kind="ExternalInput")
    wout_a = nc.dram_tensor("wout_a", [16, D, 128], F32, kind="ExternalInput")
    wqkv_b = nc.dram_tensor("wqkv_b", [48, D, 128], F32, kind="ExternalInput")
    wo_b = nc.dram_tensor("wo_b", [16, D, 128], F32, kind="ExternalInput")
    wgu = nc.dram_tensor("wgu", [8, 8, 2, D, 128], F32, kind="ExternalInput")
    wdn = nc.dram_tensor("wdn", [8, 16, 1024, 128], F32, kind="ExternalInput")
    wrt = nc.dram_tensor("wrt", [D, 8], F32, kind="ExternalInput")
    qnw_d = nc.dram_tensor("qnw", [128, 16], F32, kind="ExternalInput")
    knw_d = nc.dram_tensor("knw", [128, 16], F32, kind="ExternalInput")

    y = nc.dram_tensor("y", [D, TC], F32, kind="ExternalOutput")
    if DEBUG_TAPS:
        tap_log = nc.dram_tensor("tap_log", [128, 64], F32, kind="ExternalOutput")
        tap_dw = nc.dram_tensor("tap_dw", [128, 64], F32, kind="ExternalOutput")
        tap_ht = nc.dram_tensor("tap_ht", [128, 8 * TC], BF16, kind="ExternalOutput")
        tap_q = nc.dram_tensor("tap_q", [D, TC], BF16, kind="ExternalOutput")
        tap_kv = nc.dram_tensor("tap_kv", [2 * D, TC], BF16, kind="ExternalOutput")
        tap_o = nc.dram_tensor("tap_o", [D, TC], BF16, kind="ExternalOutput")
        tap_p = nc.dram_tensor("tap_p", [16 * 128, 512], BF16, kind="ExternalOutput")

    exkv = nc.dram_tensor("exkv", [2 * D, TC], BF16, kind="Internal")
    exk = exkv.ap()[0:D]
    exv = exkv.ap()[D:2 * D]
    exn = nc.dram_tensor("exn", [1, TC], F32, kind="Internal")
    exkvg = nc.dram_tensor("exkvg", [4 * D, TC], BF16, kind="Internal")
    exng = nc.dram_tensor("exng", [2, TC], F32, kind="Internal")
    qstash = nc.dram_tensor("qstash", [D, TC], BF16, kind="Internal")

    ctx = ExitStack()
    sb = lambda n_, s_, d_: ctx.enter_context(nc.sbuf_tensor(n_, s_, d_))
    psa = lambda n_, s_, d_: ctx.enter_context(nc.psum_tensor(n_, s_, d_))

    X = sb("X", [128, NFC * TC], F32)
    XB = sb("XB", [128, NFC * TC], BF16)
    MSK = sb("MSK", [128, 8 * 512], BF16)
    BIASV = sb("BIASV", [128, 64], F32)
    KH = [sb(f"KH{i}", [128, 2 * TC], BF16) for i in range(2)]
    VH = [sb(f"VH{i}", [128, 2 * TC], BF16) for i in range(2)]
    P = [sb(f"Pt{i}", [128, 512], BF16) for i in range(3)]
    WR = [sb(f"WRs{i}", [128, NFC * 128], BF16) for i in range(NSLOT)]
    BC = sb("BC", [128, 2 * TC], F32)
    STAT = sb("STAT", [128, 3 * TC], F32)
    RT = sb("RT", [128, TC], BF16)
    RTS = sb("RTS", [128, TC], BF16)
    RSC = sb("RSC", [128, TC], BF16)
    HT = [sb("HTt0", [128, 8 * TC], BF16)] * 2
    SIL = sb("SIL", [128, TC], BF16)
    SCK = [sb(f"SCKt{i}", [128, TC], BF16) for i in range(2)]
    SCV = sb("SCV", [128, TC], BF16)
    SQB = sb("SQB", [128, TC], BF16)
    QS = [sb(f"QSt{i}", [128, 512], BF16) for i in range(2)]
    QT = [sb(f"QTt{i}", [128, 512], BF16) for i in range(2)]
    RRK = sb("RRK", [128, 16], F32)
    LOG = sb("LOG", [128, 64], F32)
    DW = sb("DW", [128, 64], F32)
    TPR = sb("TPR", [128, 32], F32)
    WRS = sb("WRS", [128, NFC * 8], BF16)
    QNW = sb("QNW", [128, 16], F32)
    KNW = sb("KNW", [128, 16], F32)
    IDN = sb("IDN", [128, 128], F32)
    IDNB = sb("IDNB", [128, 128], BF16)
    ONE = sb("ONE", [128, 1], F32)
    ONEB = sb("ONEB", [128, 1], BF16)
    ONER = sb("ONER", [128, 128], F32)
    CST = sb("CST", [128, 3], F32)

    pb = [psa(f"pb{i}", [128, 512], F32) if i != 5 else
          psa("pb5b", [128, 1024], BF16) for i in range(8)]
    pbt = pb[5]

    sIN = kb.sem("sIN")
    sST = kb.sem("sST")
    sMSK = kb.sem("sMSK")
    sRING = [kb.sem(f"sRING{i}") for i in range(NSLOT)]
    sKH = [kb.sem(f"sKH{i}") for i in range(2)]
    sVH = [kb.sem(f"sVH{i}") for i in range(2)]
    sQSm = [kb.sem(f"sQS{i}") for i in range(2)]
    sSCK = [kb.sem(f"sSCK{i}") for i in range(2)]
    sSCV = kb.sem("sSCV")
    sEXN = kb.sem("sEXN")
    sMM = kb.sem("sMM")
    sEV = kb.sem("sEV")
    kb._sev = sEV
    sAC = kb.sem("sAC")
    kb._sac = sAC
    sGP = kb.sem("sGP")
    sCC = [kb.sem(f"sCC{i}") for i in range(4)]
    sAUX = kb.sem("sAUX")

    wseq = []
    wseq += [(wqkv_a.ap()[j], 16) for j in range(48)]
    wseq += [(wout_a.ap()[j], 16) for j in range(16)]
    ffn_seq = []
    for e in range(8):
        for ic in range(8):
            ffn_seq.append((wgu.ap()[e, ic, 0], 16))
            ffn_seq.append((wgu.ap()[e, ic, 1], 16))
        for of in range(16):
            ffn_seq.append((wdn.ap()[e, of], 8))
    FB1 = len(wseq)
    wseq += ffn_seq
    QB2 = len(wseq)
    wseq += [(wqkv_b.ap()[j], 16) for j in range(48)]
    wseq += [(wo_b.ap()[j], 16) for j in range(16)]
    FB2 = len(wseq)
    wseq += ffn_seq

    ring = {"emitted": 0, "released": [None] * len(wseq)}

    def ring_ensure(i):
        while ring["emitted"] <= min(i + PREF, len(wseq) - 1):
            j = ring["emitted"]
            src, nkc = wseq[j]
            slot = WR[j % NSLOT]
            waits = []
            if j >= NSLOT:
                rel = ring["released"][j - NSLOT]
                assert rel is not None
                waits.append(rel)
            src3 = src.rearrange("(kc p) o -> p kc o", p=128)
            dst3 = slot[:, 0:nkc * 128].rearrange("p (kc o) -> p kc o", kc=nkc)
            kb.gp(lambda e, d=dst3, s=src3: e.dma_start(out=d, in_=s),
                  waits=waits, inc=sRING[j % NSLOT])
            ring["emitted"] += 1

    def ring_use(i):
        ring_ensure(i)
        nuse = i // NSLOT + 1
        return WR[i % NSLOT], (sRING[i % NSLOT], 16 * nuse)

    def ring_release(i):
        ring["released"][i] = (sMM, sMM.n)

    for fc in range(NFC):
        kb.io(lambda e, fc=fc: e.dma_start(out=X[:, fc * TC:(fc + 1) * TC],
                                           in_=x0[fc * 128:(fc + 1) * 128, :]), inc=sIN)
    kb.io(lambda e: e.dma_start(out=IDN[:, :], in_=ident[:, :]), inc=sIN)
    kb.io(lambda e: e.dma_start(out=QNW[:, :], in_=qnw_d[:, :]), inc=sIN)
    kb.io(lambda e: e.dma_start(out=KNW[:, :], in_=knw_d[:, :]), inc=sIN)
    kb.io(lambda e: e.dma_start(out=BIASV[:, :], in_=biasv_d[:, :]), inc=sIN)
    kb.gp(lambda e: e.dma_start(out=RT[:, :], in_=ropec_d[:, :]), inc=sAUX)
    kb.gp(lambda e: e.dma_start(out=RTS[:, :], in_=ropes_d[:, :]), inc=sAUX)
    kb.gp(lambda e: e.dma_start(
        out=WRS[:, :].rearrange("p (kc o) -> p kc o", kc=NFC),
        in_=wrt.ap().rearrange("(kc p) o -> p kc o", p=128)), inc=sAUX)
    kb.gp(lambda e: e.memset(ONE[:, :], 1.0), inc=sGP, incval=1)
    kb.gp(lambda e: e.memset(ONEB[:, :], 1.0), inc=sGP, incval=1)
    kb.gp(lambda e: e.memset(ONER[:, :], 1.0), inc=sGP, incval=1)
    kb.gp(lambda e: e.memset(CST[:, 0:1], SHIFT), inc=sGP, incval=1)
    kb.gp(lambda e: e.memset(CST[:, 1:2], 1e-5), inc=sGP, incval=1)
    kb.gp(lambda e: e.memset(CST[:, 2:3], 1e-6), inc=sGP, incval=1)
    pre_in = (sIN, sIN.n)
    pre_gp = (sGP, sGP.n)
    pre_aux = (sAUX, sAUX.n)
    kb.dve(lambda e: e.tensor_copy(IDNB[:, :], IDN[:, :]), waits=[pre_in, pre_gp],
           inc=sEV)

    bankfree = [(sEV, 0)] * 8
    sck_rel = [(sEV, 0), (sEV, 0)]
    kh_rel = [(sEV, 0), (sEV, 0)]
    vh_rel = [(sEV, 0), (sEV, 0)]
    qs_rel = [(sEV, 0), (sEV, 0)]
    qt_rel = [(sEV, 0), (sEV, 0)]
    ht_rel = [(sEV, 0), (sEV, 0)]
    msk_rel = [(sEV, 0)]
    scv_rel = [(sEV, 0)]

    def emit_ln(eps=1e-5):
        x_ready = (sEV, sEV.n)
        for tc in range(2):
            for fc in range(NFC):
                xs = X[:, fc * TC + tc * 512: fc * TC + tc * 512 + 512]
                w = [x_ready, pre_gp, bankfree[3 + tc]] if fc == 0 else ()
                kb.pe(lambda e, xs=xs, tc=tc, fc=fc: e.matmul(
                    pb[3 + tc][0:1, :], ONE[:, :], xs,
                    start=(fc == 0), stop=(fc == NFC - 1)),
                    waits=w, inc=sMM if fc == NFC - 1 else None)
        sq_rel = {0: (sMM, 0), 1: (sMM, 0)}
        for tc in range(2):
            for fc in range(NFC):
                xs = X[:, fc * TC + tc * 512: fc * TC + tc * 512 + 512]
                sq = SQB[:, (fc % 2) * 512:(fc % 2) * 512 + 512]
                wsq = [sq_rel[fc % 2]]
                if tc == 0 and fc == 0:
                    wsq.append(x_ready)
                kb.act(lambda e, sq=sq, xs=xs: e.activation(sq, xs, AF.Square),
                       waits=wsq, inc=sAC)
                w = [(sAC, sAC.n)]
                if fc == 0:
                    w.append(bankfree[6 + tc])
                kb.pe(lambda e, sq=sq, tc=tc, fc=fc: e.matmul(
                    pb[6 + tc][0:1, :], ONEB[:, :], sq,
                    start=(fc == 0), stop=(fc == NFC - 1)),
                    waits=w, inc=sMM)
                sq_rel[fc % 2] = (sMM, sMM.n)
        mm_done = (sMM, sMM.n)
        for tc in range(2):
            sl = slice(tc * 512, tc * 512 + 512)
            sl1 = slice(TC + tc * 512, TC + tc * 512 + 512)
            kb.dve(lambda e, sl=sl, tc=tc: e.tensor_scalar(
                STAT[0:1, sl], pb[3 + tc][0:1, :], 1.0 / D, None, ALU.mult),
                waits=[mm_done], inc=sEV)
            kb.dve(lambda e, sl1=sl1, tc=tc: e.tensor_scalar(
                STAT[0:1, sl1], pb[6 + tc][0:1, :], 1.0 / D, None, ALU.mult),
                inc=sEV)
        kb.dve(lambda e: e.engine_nop(), waits=[(sEV, sEV.n)])
        kb.dve(lambda e: e.tensor_tensor(
            STAT[0:1, 2 * TC:3 * TC], STAT[0:1, 0:TC], STAT[0:1, 0:TC],
            ALU.mult), inc=sEV)
        kb.dve(lambda e: e.tensor_tensor(
            STAT[0:1, TC:2 * TC], STAT[0:1, TC:2 * TC],
            STAT[0:1, 2 * TC:3 * TC], ALU.subtract), inc=sEV)
        for i in (3, 4, 6, 7):
            bankfree[i] = (sEV, sEV.n)
        kb.act(lambda e: e.activation(STAT[0:1, 2 * TC:3 * TC],
                                      STAT[0:1, TC:2 * TC], AF.Ln,
                                      bias=CST[0:1, 1:2]),
               waits=[(sEV, sEV.n), pre_gp], inc=sAC)
        kb.act(lambda e: e.activation(STAT[0:1, 2 * TC:3 * TC],
                                      STAT[0:1, 2 * TC:3 * TC], AF.Exp,
                                      scale=-0.5), inc=sAC)
        inv_done = (sAC, sAC.n)
        for tc in range(2):
            sl = slice(tc * 512, tc * 512 + 512)
            for (so, do) in ((0, 0), (2 * TC, TC)):
                kb.pe(lambda e, sl=sl, so=so, tc=tc: e.matmul(
                    pb[4][:, :], ONER[0:1, :],
                    STAT[0:1, so + tc * 512: so + tc * 512 + 512],
                    start=True, stop=True),
                    waits=[inv_done, bankfree[4]], inc=sMM)
                kb.dve(lambda e, sl=sl, do=do, tc=tc: e.tensor_copy(
                    BC[:, do + tc * 512: do + tc * 512 + 512], pb[4][:, :]),
                    waits=[(sMM, sMM.n)], inc=sEV)
                bankfree[4] = (sEV, sEV.n)
        for fc in range(NFC):
            xs = X[:, fc * TC:(fc + 1) * TC]
            xb = XB[:, fc * TC:(fc + 1) * TC]
            kb.dve(lambda e, xb=xb, xs=xs: e.tensor_tensor(
                xb, xs, BC[:, 0:TC], ALU.subtract), inc=sEV)
            kb.dve(lambda e, xb=xb: e.tensor_tensor(
                xb, xb, BC[:, TC:2 * TC], ALU.mult), inc=sEV)
        return (sEV, sEV.n)

    nbk = {"i": 0}

    def two_banks():
        i = nbk["i"] % 2
        nbk["i"] += 1
        return [2 * i, 2 * i + 1]

    def proj_block(block_i, ln_done, nkc=16):
        slot, wv = ring_use(block_i)
        bks = two_banks()
        out = []
        for tc in range(2):
            b = bks[tc]
            for kc in range(nkc):
                lhs = slot[:, kc * 128:(kc + 1) * 128]
                rhs = XB[:, kc * TC + tc * 512: kc * TC + tc * 512 + 512]
                w = [wv, ln_done, bankfree[b]] if kc == 0 else ()
                kb.pe(lambda e, b=b, lhs=lhs, rhs=rhs, kc=kc, nkc=nkc: e.matmul(
                    pb[b][:, :], lhs, rhs, start=(kc == 0), stop=(kc == nkc - 1)),
                    waits=w, inc=sMM if kc == nkc - 1 else None)
            out.append((b, (sMM, sMM.n)))
        ring_release(block_i)
        return out

    def rope_chunk(buf, c0, T, rt_off):
        x1, x2 = buf[0:64, c0:c0 + T], buf[64:128, c0:c0 + T]
        s1, s2 = RSC[0:64, 0:T], RSC[64:128, 0:T]
        sl = slice(rt_off, rt_off + T)
        kb.dve(lambda e: e.tensor_tensor(s1, x2, RTS[64:128, sl], ALU.mult),
               waits=[pre_aux], inc=sEV)
        kb.dve(lambda e: e.tensor_tensor(s2, x1, RTS[0:64, sl], ALU.mult), inc=sEV)
        kb.dve(lambda e: e.tensor_tensor(x1, x1, RT[0:64, sl], ALU.mult), inc=sEV)
        kb.dve(lambda e: e.tensor_tensor(x2, x2, RT[64:128, sl], ALU.mult), inc=sEV)
        kb.dve(lambda e: e.tensor_tensor(x1, x1, s1, ALU.subtract), inc=sEV)
        kb.dve(lambda e: e.tensor_tensor(x2, x2, s2, ALU.add), inc=sEV)

    def emit_qkv(layer, base, ln_done, exchange_fn=None):
        isb = (layer == 1)
        sqkv_rel = {0: (sMM, 0), 1: (sMM, 0)}
        for h in range(H):
            res = proj_block(base + h, ln_done)
            sc = SCK[h % 2]
            for tc in range(2):
                b, mk = res[tc]
                kb.dve(lambda e, sc=sc, tc=tc, b=b: e.tensor_copy(
                    sc[:, tc * 512:tc * 512 + 512], pb[b][:, :]),
                    waits=[mk, sck_rel[h % 2]] if tc == 0 else [mk], inc=sEV)
                bankfree[b] = (sEV, sEV.n)
            if isb:
                for tc in range(2):
                    sl = slice(tc * 512, tc * 512 + 512)
                    kb.act(lambda e, sc=sc, sl=sl: e.activation(
                        SQB[:, sl], sc[:, sl], AF.Square),
                        waits=[(sEV, sEV.n), sqkv_rel[tc]], inc=sAC)
                    w = [(sAC, sAC.n)]
                    if h == 0:
                        w.append(bankfree[6 + tc])
                    kb.pe(lambda e, sl=sl, h=h, tc=tc: e.matmul(
                        pb[6 + tc][0:1, :], ONEB[:, :], SQB[:, sl],
                        start=(h == 0), stop=(h == H - 1)),
                        waits=w, inc=sMM)
                    sqkv_rel[tc] = (sMM, sMM.n)
                kb.dve(lambda e, sc=sc, h=h: e.tensor_scalar(
                    sc[:, :], sc[:, :], KNW[:, h:h + 1], None, ALU.mult),
                    waits=[(sAC, sAC.n)], inc=sEV)
                rope_chunk(sc, 0, TC, 0)
            w = [(sEV, sEV.n)]
            if layer == 1 and h == 0:
                w.append((sCC[0], 1))
            kb.io(lambda e, h=h, sc=sc: e.dma_start(
                out=exk[h * 128:(h + 1) * 128, :], in_=sc[:, :]),
                waits=w, inc=sSCK[h % 2])
            sck_rel[h % 2] = (sSCK[h % 2], sSCK[h % 2].n)
        if isb:
            ss_done = (sMM, sMM.n)
            for tc in range(2):
                sl = slice(tc * 512, tc * 512 + 512)
                kb.dve(lambda e, sl=sl, tc=tc: e.tensor_scalar(
                    STAT[0:1, sl], pb[6 + tc][0:1, :], 1.0 / D, None, ALU.mult),
                    waits=[ss_done, (sEXN, sEXN.n)], inc=sEV)
                bankfree[6 + tc] = (sEV, sEV.n)
            kb.act(lambda e: e.activation(STAT[0:1, 0:TC], STAT[0:1, 0:TC], AF.Ln,
                                          bias=CST[0:1, 2:3]),
                   waits=[(sEV, sEV.n)], inc=sAC)
            kb.act(lambda e: e.activation(STAT[0:1, 0:TC], STAT[0:1, 0:TC], AF.Exp,
                                          scale=-0.5), inc=sAC)
            kb.io(lambda e: e.dma_start(out=exn[0:1, :], in_=STAT[0:1, 0:TC]),
                  waits=[(sAC, sAC.n), (sCC[0], 1)], inc=sEXN)
        for h in range(H):
            res = proj_block(base + 16 + h, ln_done)
            sc = SCK[h % 2]
            for tc in range(2):
                b, mk = res[tc]
                kb.dve(lambda e, sc=sc, tc=tc, b=b: e.tensor_copy(
                    sc[:, tc * 512:tc * 512 + 512], pb[b][:, :]),
                    waits=[mk, sck_rel[h % 2]] if tc == 0 else [mk], inc=sEV)
                bankfree[b] = (sEV, sEV.n)
            evv = (sEV, sEV.n)
            for tt in range(8):
                w = [evv] if tt == 0 else ()
                if tt == 0:
                    w = [evv, bankfree[5]]
                kb.pe(lambda e, sc=sc, tt=tt: e.transpose(
                    pbt[:, tt * 128:(tt + 1) * 128],
                    sc[:, tt * 128:(tt + 1) * 128], IDNB[:, :]),
                    waits=w, inc=sMM if tt == 7 else None)
            tr_done = (sMM, sMM.n)
            sck_rel[h % 2] = tr_done
            kb.dve(lambda e: e.tensor_copy(SCV[:, :], pbt[:, :]),
                   waits=[tr_done, scv_rel[0]], inc=sEV)
            bankfree[5] = (sEV, sEV.n)
            w = [(sEV, sEV.n)]
            if layer == 1 and h == 0:
                w.append((sCC[0], 1))
            kb.io(lambda e, h=h: e.dma_start(
                out=exv[h * 128:(h + 1) * 128, :], in_=SCV[:, :]),
                waits=w, inc=sSCV)
            scv_rel[0] = (sSCV, sSCV.n)
        if exchange_fn is not None:
            exchange_fn()
        for h in range(H):
            res = proj_block(base + 32 + h, ln_done)
            sc = SCK[h % 2]
            for tc in range(2):
                b, mk = res[tc]
                kb.dve(lambda e, sc=sc, tc=tc, b=b: e.tensor_copy(
                    sc[:, tc * 512:tc * 512 + 512], pb[b][:, :]),
                    waits=[mk, sck_rel[h % 2]] if tc == 0 else [mk], inc=sEV)
                bankfree[b] = (sEV, sEV.n)
            if isb:
                for tc in range(2):
                    sl = slice(tc * 512, tc * 512 + 512)
                    kb.act(lambda e, sc=sc, sl=sl: e.activation(
                        SQB[:, sl], sc[:, sl], AF.Square),
                        waits=[(sEV, sEV.n), sqkv_rel[tc]], inc=sAC)
                    w = [(sAC, sAC.n)]
                    if h == 0:
                        w.append(bankfree[6 + tc])
                    kb.pe(lambda e, sl=sl, h=h, tc=tc: e.matmul(
                        pb[6 + tc][0:1, :], ONEB[:, :], SQB[:, sl],
                        start=(h == 0), stop=(h == H - 1)),
                        waits=w, inc=sMM)
                    sqkv_rel[tc] = (sMM, sMM.n)
            w = [(sEV, sEV.n)]
            if layer == 1 and h == 0:
                w.append((sQSm[0], sQSm[0].n))
                w.append((sQSm[1], sQSm[1].n))
            kb.io(lambda e, h=h, sc=sc: e.dma_start(
                out=qstash[h * 128:(h + 1) * 128, :], in_=sc[:, :]),
                waits=w, inc=sSCK[h % 2])
            sck_rel[h % 2] = (sSCK[h % 2], sSCK[h % 2].n)
        if isb:
            ss_done = (sMM, sMM.n)
            for tc in range(2):
                sl1 = slice(TC + tc * 512, TC + tc * 512 + 512)
                kb.dve(lambda e, sl1=sl1, tc=tc: e.tensor_scalar(
                    STAT[0:1, sl1], pb[6 + tc][0:1, :], 1.0 / D, None, ALU.mult),
                    waits=[ss_done], inc=sEV)
                bankfree[6 + tc] = (sEV, sEV.n)
            kb.act(lambda e: e.activation(STAT[0:1, TC:2 * TC],
                                          STAT[0:1, TC:2 * TC],
                                          AF.Ln, bias=CST[0:1, 2:3]),
                   waits=[(sEV, sEV.n)], inc=sAC)
            kb.act(lambda e: e.activation(STAT[0:1, TC:2 * TC],
                                          STAT[0:1, TC:2 * TC],
                                          AF.Exp, scale=-0.5), inc=sAC)
            kb.dve(lambda e: e.tensor_scalar(STAT[0:1, TC:2 * TC],
                                             STAT[0:1, TC:2 * TC],
                                             RQ, None, ALU.mult),
                   waits=[(sAC, sAC.n)], inc=sEV)
            for tc in range(2):
                sl = slice(tc * 512, tc * 512 + 512)
                kb.pe(lambda e, tc=tc: e.matmul(
                    pb[4][:, :], ONER[0:1, :],
                    STAT[0:1, TC + tc * 512: TC + tc * 512 + 512],
                    start=True, stop=True),
                    waits=[(sEV, sEV.n), bankfree[4]], inc=sMM)
                kb.dve(lambda e, sl=sl: e.tensor_copy(BC[:, sl], pb[4][:, :]),
                       waits=[(sMM, sMM.n)], inc=sEV)
                bankfree[4] = (sEV, sEV.n)
        return [(sSCK[0], sSCK[0].n), (sSCK[1], sSCK[1].n)]

    def emit_exchange(idx, with_n=False):
        waits = [(sSCK[0], sSCK[0].n), (sSCK[1], sSCK[1].n), (sSCV, sSCV.n)]
        if idx >= 1:
            waits += [(sKH[0], sKH[0].n), (sKH[1], sKH[1].n),
                      (sVH[0], sVH[0].n), (sVH[1], sVH[1].n),
                      (sIN, sIN.n)]
        kb.gp(lambda e: e.collective_compute(
            "AllGather", ALU.bypass,
            replica_groups=[[0, 1], [2, 3], [4, 5], [6, 7]],
            ins=[exkv.ap().opt()],
            outs=[exkvg.ap().opt()]),
            waits=waits, inc=sCC[idx], incval="default")
        if with_n:
            kb.gp(lambda e: e.collective_compute(
                "AllGather", ALU.bypass,
                replica_groups=[[0, 1], [2, 3], [4, 5], [6, 7]],
                ins=[exn.ap().opt()],
                outs=[exng.ap().opt()]),
                waits=[(sEXN, sEXN.n)], inc=sCC[idx + 1], incval="default")

    def emit_attn(layer, maskt, cc_idx, stash_done):
        isb = (layer == 1)
        ccw = (sCC[cc_idx], 1)
        if isb:
            kb.io(lambda e: e.dma_start(out=STAT[0:1, 0:TC], in_=exng[0:1, :]),
                  waits=[ccw, (sCC[cc_idx + 1], 1), (sEXN, sEXN.n),
                         (sMM, sMM.n), (sEV, sEV.n), (sAC, sAC.n)], inc=sIN)
            kb.io(lambda e: e.dma_start(out=STAT[0:1, TC:2 * TC],
                                        in_=exng[1:2, :]), inc=sIN)
            rr_in = (sIN, sIN.n)
            for kt in range(16):
                w = [rr_in, pre_in]
                if kt == 0:
                    w.append(bankfree[4])
                src = STAT[0:1, kt * 128:(kt + 1) * 128]
                kb.pe(lambda e, kt=kt, src=src: e.transpose(
                    pb[4][:, kt:kt + 1], src, IDN[0:1, 0:1]),
                    waits=w, inc=sMM if kt == 15 else None)
            kb.dve(lambda e: e.tensor_copy(RRK[:, :], pb[4][:, 0:16]),
                   waits=[(sMM, sMM.n)], inc=sEV)
            bankfree[4] = (sEV, sEV.n)

        def reload(qc, h):
            hh = h % 2
            ksrc = exkvg.ap().rearrange("(r x) t -> x r t", r=2)[
                h * 128:(h + 1) * 128]
            kdst = KH[hh][:, :].rearrange("p (r t) -> p r t", r=2)
            kb.io(lambda e, kdst=kdst, ksrc=ksrc: e.dma_start(out=kdst, in_=ksrc),
                  waits=[ccw, kh_rel[hh]], inc=sKH[hh])
            vsrc = exkvg.ap().rearrange("(r x) t -> x r t", r=2)[
                D + h * 128:D + (h + 1) * 128]
            vdst = VH[hh][:, :].rearrange("p (r t) -> p r t", r=2)
            kb.io(lambda e, vdst=vdst, vsrc=vsrc: e.dma_start(out=vdst, in_=vsrc),
                  waits=[ccw, vh_rel[hh]], inc=sVH[hh])
            kb.io(lambda e, h=h, qc=qc, hh=hh: e.dma_start(
                out=QS[hh][:, :],
                in_=qstash[h * 128:(h + 1) * 128, qc * 512:qc * 512 + 512]),
                waits=[qs_rel[hh]] + stash_done, inc=sQSm[hh])
            return (sKH[hh], sKH[hh].n), (sVH[hh], sVH[hh].n), (sQSm[hh], sQSm[hh].n)

        exp_free = [bankfree[0], bankfree[1], bankfree[2]]
        p_extra = {}
        pend = {}
        pending_norm = [None]

        def bdset(qc):
            return list(range(4 * qc, 4 * qc + 4)) + \
                list(range(4 * qc + 8, 4 * qc + 12))

        for qc in range(2):
            ktmax = 12 if qc == 0 else 16
            bset = bdset(qc)
            bidx = {kt: j for j, kt in enumerate(bset)}
            mw = [msk_rel[0]]
            for j in range(8):
                kb.io(lambda e, j=j, qc=qc: e.dma_start(
                    out=MSK[:, j * 512:(j + 1) * 512],
                    in_=maskadd.ap()[layer, qc, j]),
                    waits=mw if j == 0 else (), inc=sMSK)
                mw = ()
            msk_w = (sMSK, sMSK.n)
            if qc == 0:
                pend[0] = reload(0, 0)
            for h in range(H):
                hh = h % 2
                khw, vhw, qsw = pend[h]
                if h + 1 < H:
                    pend[h + 1] = reload(qc, h + 1)
                elif qc == 0:
                    pend[0] = reload(1, 0)
                    pend.pop(1, None)
                if isb:
                    kb.dve(lambda e, hh=hh, qc=qc: e.tensor_tensor(
                        QT[hh][:, :], QS[hh][:, :],
                        BC[:, qc * 512:qc * 512 + 512], ALU.mult),
                        waits=[qsw, qt_rel[hh]], inc=sEV)
                    kb.dve(lambda e, hh=hh, h=h: e.tensor_scalar(
                        QT[hh][:, :], QT[hh][:, :], QNW[:, h:h + 1], None,
                        ALU.mult), inc=sEV)
                    rope_chunk(QT[hh], 0, 512, qc * 512)
                    qs_rel[hh] = (sEV, sEV.n)
                    qrhs = QT[hh]
                else:
                    qrhs = QS[hh]
                avb = 3 if h % 2 == 0 else 4
                smb = 6 if h % 2 == 0 else 7
                pend_av = []
                for kt in range(ktmax):
                    sbk = kt % 3
                    w = [exp_free[sbk]]
                    if kt < 3:
                        w.append(khw)
                        if not isb:
                            w.append(qsw)
                        else:
                            w.append((sEV, sEV.n))
                    isbd = kt in bidx
                    kb.pe(lambda e, sbk=sbk, kt=kt, qrhs=qrhs, hh=hh,
                          isbd=isbd: e.matmul(
                              pb[sbk][:, :], KH[hh][:, kt * 128:(kt + 1) * 128],
                              qrhs[:, :], start=True, stop=not isbd),
                          waits=w, inc=None if isbd else sMM)
                    if isbd:
                        j = bidx[kt]
                        kb.pe(lambda e, sbk=sbk, j=j: e.matmul(
                            pb[sbk][:, :], IDNB[:, :],
                            MSK[:, j * 512:(j + 1) * 512],
                            start=False, stop=True),
                            waits=[msk_w], inc=sMM)
                    scl = RRK[:, kt:kt + 1] if isb else RQ
                    bia = BIASV[:, layer * 32 + qc * 16 + kt:
                                layer * 32 + qc * 16 + kt + 1]
                    wexp = [(sMM, sMM.n), p_free[sbk]]
                    kb.act(lambda e, sbk=sbk, scl=scl, bia=bia: e.activation(
                        P[sbk][:, :], pb[sbk][:, :], AF.Exp,
                        bias=bia, scale=scl),
                        waits=wexp, inc=sAC)
                    exp_free[sbk] = (sAC, sAC.n)
                    pmask = (sAC, sAC.n)

                    def mk_av(sbk=sbk, kt=kt, hh=hh, avb=avb, smb=smb,
                              ktmax=ktmax, pmask=pmask, vhw=vhw):
                        def av():
                            w2 = [pmask]
                            if kt == 0:
                                w2 += [vhw, bankfree[avb], bankfree[smb]]
                            kb.pe(lambda e: e.matmul(
                                pb[avb][:, :],
                                VH[hh][:, kt * 128:(kt + 1) * 128],
                                P[sbk][:, :], start=(kt == 0),
                                stop=(kt == ktmax - 1)),
                                waits=w2)
                            kb.pe(lambda e: e.matmul(
                                pb[smb][0:1, :], ONEB[:, :], P[sbk][:, :],
                                start=(kt == 0), stop=(kt == ktmax - 1)),
                                inc=sMM)
                            p_free[sbk] = (sMM, sMM.n)
                        return av
                    pend_av.append(mk_av())
                    if len(pend_av) > 2:
                        pend_av.pop(0)()
                while pend_av:
                    pend_av.pop(0)()
                av_done = (sMM, sMM.n)
                kh_rel[hh] = av_done
                vh_rel[hh] = av_done
                if not isb:
                    qs_rel[hh] = av_done
                else:
                    qt_rel[hh] = av_done
                if pending_norm[0] is not None:
                    pending_norm[0]()

                def mk_norm(h=h, qc=qc, avb=avb, smb=smb, av_done=av_done):
                    def norm():
                        kb.dve(lambda e: e.reciprocal(
                            STAT[0:1, 2 * TC:2 * TC + 512], pb[smb][0:1, :]),
                            waits=[av_done], inc=sEV)
                        bankfree[smb] = (sEV, sEV.n)
                        kb.pe(lambda e: e.matmul(
                            pb[smb][:, :], ONER[0:1, :],
                            STAT[0:1, 2 * TC:2 * TC + 512],
                            start=True, stop=True),
                            waits=[(sEV, sEV.n)], inc=sMM)
                        kb.dve(lambda e: e.tensor_copy(
                            BC[:, 3 * 512:4 * 512], pb[smb][:, :]),
                            waits=[(sMM, sMM.n)], inc=sEV)
                        bankfree[smb] = (sEV, sEV.n)
                        kb.dve(lambda e: e.tensor_tensor(
                            XB[:, h * TC + qc * 512: h * TC + qc * 512 + 512],
                            pb[avb][:, :], BC[:, 3 * 512:4 * 512], ALU.mult),
                            waits=[av_done], inc=sEV)
                        bankfree[avb] = (sEV, sEV.n)
                    return norm
                pending_norm[0] = mk_norm()
            msk_rel[0] = (sMM, sMM.n)
        pending_norm[0]()
        for i in range(3):
            bankfree[i] = exp_free[i]

    def emit_outproj(base):
        o_ready = (sEV, sEV.n)
        for of in range(NFC):
            res = proj_block(base + of, o_ready)
            for tc in range(2):
                b, mk = res[tc]
                xsl = X[:, of * TC + tc * 512: of * TC + tc * 512 + 512]
                kb.dve(lambda e, xsl=xsl, b=b: e.tensor_tensor(
                    xsl, pb[b][:, :], xsl, ALU.add), waits=[mk], inc=sEV)
                bankfree[b] = (sEV, sEV.n)

    def emit_ffn(base, ln_done):
        sil_rel = [(sEV, 0), (sEV, 0)]
        for tt in range(8):
            b = 3 + tt % 2
            for fc in range(NFC):
                w = []
                if fc == 0:
                    w = [ln_done, pre_aux, bankfree[b]]
                kb.pe(lambda e, b=b, fc=fc, tt=tt: e.matmul(
                    pb[b][0:128, 0:8],
                    XB[:, fc * TC + tt * 128: fc * TC + tt * 128 + 128],
                    WRS[:, fc * 8:(fc + 1) * 8],
                    start=(fc == 0), stop=(fc == NFC - 1)),
                    waits=w, inc=sMM if fc == NFC - 1 else None)
            kb.dve(lambda e, b=b, tt=tt: e.tensor_copy(
                LOG[:, tt * 8:(tt + 1) * 8], pb[b][0:128, 0:8]),
                waits=[(sMM, sMM.n)], inc=sEV)
            bankfree[b] = (sEV, sEV.n)
        def dves(fn):
            kb.dve(fn, waits=[(sEV, sEV.n)], inc=sEV)
        for tt in range(8):
            lg = LOG[:, tt * 8:(tt + 1) * 8]
            dves(lambda e, lg=lg: e.reduce_max(TPR[:, 0:1], lg, axis=AX.X))
            dves(lambda e, lg=lg: e.tensor_scalar(
                TPR[:, 4:12], lg, TPR[:, 0:1], None, ALU.is_equal))
            dves(lambda e: e.tensor_scalar(
                TPR[:, 20:28], TPR[:, 4:12], -1e30, None, ALU.mult))
            dves(lambda e, lg=lg: e.tensor_tensor(
                TPR[:, 20:28], TPR[:, 20:28], lg, ALU.add))
            dves(lambda e: e.reduce_max(TPR[:, 1:2], TPR[:, 20:28], axis=AX.X))
            dves(lambda e, lg=lg: e.tensor_scalar(
                TPR[:, 12:20], lg, TPR[:, 1:2], None, ALU.is_equal))
            dves(lambda e: e.tensor_tensor(
                TPR[:, 2:3], TPR[:, 1:2], TPR[:, 0:1], ALU.subtract))
            kb.act(lambda e: e.activation(TPR[:, 2:3], TPR[:, 2:3], AF.Exp),
                   waits=[(sEV, sEV.n)], inc=sAC)
            kb.dve(lambda e: e.tensor_scalar(
                TPR[:, 28:29], TPR[:, 2:3], 1.0, None, ALU.add),
                waits=[(sAC, sAC.n)], inc=sEV)
            dves(lambda e: e.reciprocal(TPR[:, 2:3], TPR[:, 28:29]))
            dves(lambda e: e.tensor_scalar(
                TPR[:, 3:4], TPR[:, 2:3], -1.0, 1.0, ALU.mult, ALU.add))
            dves(lambda e: e.tensor_scalar(
                TPR[:, 4:12], TPR[:, 4:12], TPR[:, 2:3], None, ALU.mult))
            dves(lambda e: e.tensor_scalar(
                TPR[:, 12:20], TPR[:, 12:20], TPR[:, 3:4], None, ALU.mult))
            dves(lambda e, tt=tt: e.tensor_tensor(
                DW[:, tt * 8:(tt + 1) * 8], TPR[:, 4:12], TPR[:, 12:20], ALU.add))
        dw_done = (sEV, sEV.n)
        if DEBUG_TAPS:
            kb.io(lambda e: e.dma_start(out=tap_log.ap(), in_=LOG[:, :]),
                  waits=[dw_done], inc=sST)
            kb.io(lambda e: e.dma_start(out=tap_dw.ap(), in_=DW[:, :]),
                  waits=[dw_done], inc=sST)
        for e_ in range(8):
            for tc in range(2):
                for ttl in range(4):
                    tt = tc * 4 + ttl
                    w = [dw_done, pre_in]
                    if ttl == 0:
                        w.append(bankfree[7])
                    kb.pe(lambda e, tt=tt, ttl=ttl, e_=e_: e.transpose(
                        pb[7][0:1, ttl * 128:(ttl + 1) * 128],
                        DW[:, tt * 8 + e_: tt * 8 + e_ + 1], IDN[:, :]),
                        waits=w, inc=sMM if ttl == 3 else None)
                kb.dve(lambda e, tc=tc: e.tensor_copy(
                    STAT[0:1, 2 * TC + tc * 512:2 * TC + tc * 512 + 512],
                    pb[7][0:1, 0:512]),
                    waits=[(sMM, sMM.n)], inc=sEV)
                bankfree[7] = (sEV, sEV.n)
                kb.pe(lambda e, tc=tc: e.matmul(
                    pb[6][:, :], ONER[0:1, :],
                    STAT[0:1, 2 * TC + tc * 512:2 * TC + tc * 512 + 512],
                    start=True, stop=True),
                    waits=[(sEV, sEV.n), bankfree[6]], inc=sMM)
                kb.dve(lambda e, tc=tc: e.tensor_copy(
                    SQB[:, tc * 512:tc * 512 + 512], pb[6][:, :]),
                    waits=[(sMM, sMM.n)], inc=sEV)
                bankfree[6] = (sEV, sEV.n)
            bw_done = (sEV, sEV.n)
            for ic in range(8):
                gi = base + e_ * 32 + 2 * ic
                slotg, wg = ring_use(gi)
                for tc in range(2):
                    for kc in range(NFC):
                        w = [wg, ln_done, bankfree[tc]] if kc == 0 else ()
                        kb.pe(lambda e, tc=tc, kc=kc, slotg=slotg: e.matmul(
                            pb[tc][:, :], slotg[:, kc * 128:(kc + 1) * 128],
                            XB[:, kc * TC + tc * 512: kc * TC + tc * 512 + 512],
                            start=(kc == 0), stop=(kc == NFC - 1)),
                            waits=w, inc=sMM if kc == NFC - 1 else None)
                    kb.act(lambda e, tc=tc: e.activation(
                        SIL[:, tc * 512:tc * 512 + 512], pb[tc][:, :], AF.Silu),
                        waits=[(sMM, sMM.n), sil_rel[tc]], inc=sAC)
                    bankfree[tc] = (sAC, sAC.n)
                ring_release(gi)
                ui = gi + 1
                slotu, wu = ring_use(ui)
                for tc in range(2):
                    for kc in range(NFC):
                        w = [wu, bankfree[2 + tc]] if kc == 0 else ()
                        kb.pe(lambda e, tc=tc, kc=kc, slotu=slotu: e.matmul(
                            pb[2 + tc][:, :], slotu[:, kc * 128:(kc + 1) * 128],
                            XB[:, kc * TC + tc * 512: kc * TC + tc * 512 + 512],
                            start=(kc == 0), stop=(kc == NFC - 1)),
                            waits=w, inc=sMM if kc == NFC - 1 else None)
                    hsl = HT[e_ % 2][:, ic * TC + tc * 512: ic * TC + tc * 512 + 512]
                    w = [(sMM, sMM.n), (sAC, sAC.n)]
                    if ic == 0 and tc == 0:
                        w.append(ht_rel[0])
                        w.append(ht_rel[1])
                    kb.dve(lambda e, hsl=hsl, tc=tc: e.tensor_tensor(
                        hsl, pb[2 + tc][:, :], SIL[:, tc * 512:tc * 512 + 512],
                        ALU.mult), waits=w, inc=sEV)
                    sil_rel[tc] = (sEV, sEV.n)
                    kb.dve(lambda e, hsl=hsl, tc=tc: e.tensor_tensor(
                        hsl, hsl, SQB[:, tc * 512:tc * 512 + 512], ALU.mult),
                        waits=[bw_done], inc=sEV)
                    bankfree[2 + tc] = (sEV, sEV.n)
                ring_release(ui)
            ht_done = (sEV, sEV.n)
            if DEBUG_TAPS and e_ == 0 and base == FB1:
                kb.io(lambda e: e.dma_start(out=tap_ht.ap(), in_=HT[0][:, :]),
                      waits=[ht_done], inc=sST)
            bankfree[5] = ht_done
            for of in range(NFC):
                di = base + e_ * 32 + 16 + of
                slotd, wd = ring_use(di)
                for tc in range(2):
                    bd = 4 if (of * 2 + tc) % 2 == 0 else 7
                    for kc in range(8):
                        w = [wd, ht_done, bankfree[bd]] if kc == 0 else ()
                        kb.pe(lambda e, bd=bd, kc=kc, tc=tc, slotd=slotd,
                              e_=e_: e.matmul(
                                  pb[bd][:, :], slotd[:, kc * 128:(kc + 1) * 128],
                                  HT[e_ % 2][:, kc * TC + tc * 512:
                                             kc * TC + tc * 512 + 512],
                                  start=(kc == 0), stop=(kc == 7)),
                              waits=w, inc=sMM if kc == 7 else None)
                    xsl = X[:, of * TC + tc * 512: of * TC + tc * 512 + 512]
                    kb.dve(lambda e, xsl=xsl, bd=bd: e.tensor_tensor(
                        xsl, pb[bd][:, :], xsl, ALU.add),
                        waits=[(sMM, sMM.n)], inc=sEV)
                    bankfree[bd] = (sEV, sEV.n)
                ring_release(di)
            ht_rel[0] = ht_rel[1] = (sMM, sMM.n)

    p_free = [(sMM, 0), (sMM, 0), (sMM, 0)]

    ln1 = emit_ln()
    stash1 = emit_qkv(0, 0, ln1, exchange_fn=lambda: emit_exchange(0))
    if NSUB >= 1:
        emit_attn(0, None, 0, stash1)
        if DEBUG_TAPS and NSUB == 1:
            ev = (sEV, sEV.n)
            for fc in range(NFC):
                kb.io(lambda e, fc=fc: e.dma_start(
                    out=tap_o[fc * 128:(fc + 1) * 128, :],
                    in_=XB[:, fc * TC:(fc + 1) * TC]),
                    waits=[ev] if fc == 0 else (), inc=sST)
            kb.io(lambda e: e.dma_start(out=tap_q.ap(), in_=qstash.ap()), inc=sST)
            kb.io(lambda e: e.dma_start(out=tap_kv.ap(), in_=exkv.ap()), inc=sST)
        emit_outproj(48)
    if NSUB >= 2:
        ln2 = emit_ln()
        if DEBUG_TAPS:
            for fc in range(NFC):
                kb.io(lambda e, fc=fc: e.dma_start(
                    out=tap_o[fc * 128:(fc + 1) * 128, :],
                    in_=XB[:, fc * TC:(fc + 1) * TC]),
                    waits=[ln2] if fc == 0 else (), inc=sST)
        emit_ffn(FB1, ln2)
    if NSUB >= 3:
        ln3 = emit_ln()
        stash2 = emit_qkv(1, QB2, ln3,
                          exchange_fn=lambda: emit_exchange(1, with_n=True))
        emit_attn(1, None, 1, stash2)
        emit_outproj(QB2 + 48)
    if NSUB >= 4:
        ln4 = emit_ln()
        emit_ffn(FB2, ln4)

    fin = (sEV, sEV.n)
    for fc in range(NFC):
        kb.io(lambda e, fc=fc: e.dma_start(out=y[fc * 128:(fc + 1) * 128, :],
                                           in_=X[:, fc * TC:(fc + 1) * TC]),
              waits=[fin] if fc == 0 else (), inc=sST)
    kb.emit("sync", lambda e: None, waits=[(sST, sST.n)])

    kb.check()

    for s in kb.sems:
        s.h = ctx.enter_context(nc.semaphore(s.name))
    with nc.Block() as block:
        for eng in KB.ENGINES:
            getattr(block, eng)(
                (lambda ops: lambda e: [f(e) for f in ops])(kb.ops[eng]))
    ctx.close()
    return nc


def _host_prepare(inputs):
    f32 = np.float32
    bf = ml_dtypes.bfloat16
    wqa = np.asarray(inputs["L0_attn_a_w_qkv"], f32)
    woa = np.asarray(inputs["L0_attn_a_w_out"], f32)
    wqb = np.asarray(inputs["L0_attn_b_w_qkv"], f32).T
    wob = np.asarray(inputs["L0_attn_b_w_o"], f32).T
    gu = np.asarray(inputs["L0_ffn_expert_gate_up"], f32)
    dn = np.asarray(inputs["L0_ffn_expert_down"], f32)
    rg = np.asarray(inputs["L0_ffn_router_gate"], f32)
    qnw = np.asarray(inputs["L0_attn_b_q_norm_w"], f32)
    knw = np.asarray(inputs["L0_attn_b_k_norm_w"], f32)

    def qkv_blocks(w):
        cols = list(range(16, 32)) + list(range(32, 48)) + list(range(0, 16))
        return np.ascontiguousarray(
            np.stack([w[:, j * 128:(j + 1) * 128] for j in cols], axis=0))

    def of_blocks(w, n):
        return np.ascontiguousarray(
            np.stack([w[:, j * 128:(j + 1) * 128] for j in range(n)], axis=0))

    shared = dict(
        wqkv_a=qkv_blocks(wqa),
        wout_a=of_blocks(woa, 16),
        wqkv_b=qkv_blocks(wqb),
        wo_b=of_blocks(wob, 16),
        wgu=np.ascontiguousarray(
            np.stack([np.stack([np.stack([
                gu[e][:, 0::2][:, ic * 128:(ic + 1) * 128],
                gu[e][:, 1::2][:, ic * 128:(ic + 1) * 128]], 0)
                for ic in range(8)], 0) for e in range(8)], 0)),
        wdn=np.ascontiguousarray(
            np.stack([np.stack([dn[e][:, j * 128:(j + 1) * 128]
                                for j in range(16)], 0) for e in range(8)], 0)),
        wrt=np.ascontiguousarray(rg.T),
        qnw=np.ascontiguousarray(qnw.reshape(16, 128).T),
        knw=np.ascontiguousarray(knw.reshape(16, 128).T),
        ident=np.eye(128, dtype=f32),
    )

    x = np.asarray(inputs["L0_attn_a_x"], f32)
    inv_freq = 1.0 / (10000.0 ** (np.arange(0, 128, 2) / 128.0))

    per_core = []
    for c in range(8):
        b, p = c // 2, c % 2
        pos = np.arange(p * 1024, (p + 1) * 1024, dtype=np.float64)
        fr = pos[None, :] * inv_freq[:, None]
        ropec = np.concatenate([np.cos(fr), np.cos(fr)], 0).astype(f32)
        ropes = np.concatenate([np.sin(fr), np.sin(fr)], 0).astype(f32)
        qa = p * 1024 + np.arange(1024)
        ka = np.arange(2048)
        allow_a = (ka[:, None] <= qa[None, :])
        allow_b = allow_a & (qa[None, :] - ka[:, None] <= 1023)
        KILL = np.float32(-1e6)
        SHIFT_ = np.float32(-20.0)
        maskadd = np.zeros((2, 2, 8, 128, 512), f32)
        biasv = np.zeros((128, 64), f32)
        for li, allow in ((0, allow_a), (1, allow_b)):
            for qc in range(2):
                bset = list(range(4 * qc, 4 * qc + 4)) + \
                    list(range(4 * qc + 8, 4 * qc + 12))
                qsl = slice(qc * 512, qc * 512 + 512)
                for j, kt in enumerate(bset):
                    al = allow[kt * 128:(kt + 1) * 128, qsl]
                    maskadd[li, qc, j] = np.where(al, 0.0, KILL)
                for kt in range(16):
                    col = li * 32 + qc * 16 + kt
                    if kt in bset:
                        biasv[:, col] = SHIFT_
                    else:
                        al = allow[kt * 128:(kt + 1) * 128, qsl]
                        full = al.all()
                        none = not al.any()
                        assert full or none, (li, qc, kt, p)
                        biasv[:, col] = SHIFT_ if full else (SHIFT_ + KILL)
        d = dict(shared)
        d["x0"] = np.ascontiguousarray(x[b, p * 1024:(p + 1) * 1024, :].T)
        d["ropec"] = ropec
        d["ropes"] = ropes
        d["maskadd"] = maskadd.astype(bf)
        d["biasv"] = biasv
        per_core.append(d)
    return per_core


def kernel(**inputs):
    per_core = _host_prepare(inputs)
    nc = build_program()
    res = run_bass_kernel_spmd(nc, per_core, core_ids=list(range(8)))
    out = np.empty((B, S, D), np.float32)
    for c in range(8):
        b, p = c // 2, c % 2
        out[b, p * 1024:(p + 1) * 1024, :] = res.results[c]["y"].T
    return out


if __name__ == "__main__":
    nc = build_program()
    print("n_inst:", nc.n_instructions())


# version 11
# speedup vs baseline: 1.5602x; 1.0166x over previous
import sys
sys.path.insert(0, '/opt/trn_rl_repo')
import numpy as np
import ml_dtypes
from contextlib import ExitStack
import concourse.bass as bass
import concourse.mybir as mybir
from concourse.bass_utils import run_bass_kernel_spmd

F32 = mybir.dt.float32
BF16 = mybir.dt.bfloat16
AF = mybir.ActivationFunctionType
ALU = mybir.AluOpType
AX = mybir.AxisListType

B, S, D, H = 4, 2048, 2048, 16
TC = 1024
NFC = 16
RQ = float(1.0 / np.sqrt(128.0))
SHIFT = -20.0
NSLOT = 4
PREF = 3
NSUB = 4
DEBUG_TAPS = False


class Sem:
    def __init__(self, name):
        self.name, self.n, self.h = name, 0, None


class KB:
    ENGINES = ("sync", "gpsimd", "vector", "scalar", "tensor")

    def __init__(self):
        self.ops = {e: [] for e in self.ENGINES}
        self.sems = []
        self.audit = []

    def sem(self, name):
        s = Sem(name)
        self.sems.append(s)
        return s

    def emit(self, eng, fn, waits=(), inc=None, incval=1):
        waits = tuple(waits)
        for s, v in waits:
            self.audit.append((s, v, eng))

        def closure(e, fn=fn, waits=waits, inc=inc, incval=incval):
            for s, v in waits:
                if v > 0:
                    e.wait_ge(s.h, v)
            inst = fn(e)
            if inc is not None:
                if incval == "default":
                    inst.then_inc(inc.h)
                else:
                    inst.then_inc(inc.h, incval)
        self.ops[eng].append(closure)
        if inc is not None:
            inc.n += 1 if incval == "default" else incval

    def pe(self, fn, waits=(), inc=None):
        self.emit("tensor", fn, waits, inc)

    def dve(self, fn, waits=(), inc=None):
        sev = self._sev
        waits = list(waits) + [(sev, sev.n)]
        self.emit("vector", fn, waits, inc)

    def act(self, fn, waits=(), inc=None):
        sac = self._sac
        waits = list(waits) + [(sac, sac.n)]
        self.emit("scalar", fn, waits, inc)

    def io(self, fn, waits=(), inc=None):
        self.emit("sync", fn, waits, inc, incval=16)

    def gp(self, fn, waits=(), inc=None, incval=16):
        self.emit("gpsimd", fn, waits, inc, incval=incval)

    def check(self):
        for s, v, eng in self.audit:
            assert v <= s.n, f"wait on {s.name} for {v} (max {s.n}) from {eng}"


def build_program():
    nc = bass.Bass("TRN2", target_bir_lowering=False)
    kb = KB()

    x0 = nc.dram_tensor("x0", [D, TC], F32, kind="ExternalInput")
    ident = nc.dram_tensor("ident", [128, 128], F32, kind="ExternalInput")
    ropec_d = nc.dram_tensor("ropec", [128, TC], F32, kind="ExternalInput")
    ropes_d = nc.dram_tensor("ropes", [128, TC], F32, kind="ExternalInput")
    maskadd = nc.dram_tensor("maskadd", [2, 2, 8, 128, 512], BF16,
                             kind="ExternalInput")
    biasv_d = nc.dram_tensor("biasv", [128, 64], F32, kind="ExternalInput")
    wqkv_a = nc.dram_tensor("wqkv_a", [48, D, 128], F32, kind="ExternalInput")
    wout_a = nc.dram_tensor("wout_a", [16, D, 128], F32, kind="ExternalInput")
    wqkv_b = nc.dram_tensor("wqkv_b", [48, D, 128], F32, kind="ExternalInput")
    wo_b = nc.dram_tensor("wo_b", [16, D, 128], F32, kind="ExternalInput")
    wgu = nc.dram_tensor("wgu", [8, 8, 2, D, 128], F32, kind="ExternalInput")
    wdn = nc.dram_tensor("wdn", [8, 16, 1024, 128], F32, kind="ExternalInput")
    wrt = nc.dram_tensor("wrt", [D, 8], F32, kind="ExternalInput")
    qnw_d = nc.dram_tensor("qnw", [128, 16], F32, kind="ExternalInput")
    knw_d = nc.dram_tensor("knw", [128, 16], F32, kind="ExternalInput")

    y = nc.dram_tensor("y", [D, TC], F32, kind="ExternalOutput")
    if DEBUG_TAPS:
        tap_log = nc.dram_tensor("tap_log", [128, 64], F32, kind="ExternalOutput")
        tap_dw = nc.dram_tensor("tap_dw", [128, 64], F32, kind="ExternalOutput")
        tap_ht = nc.dram_tensor("tap_ht", [128, 8 * TC], BF16, kind="ExternalOutput")
        tap_q = nc.dram_tensor("tap_q", [D, TC], BF16, kind="ExternalOutput")
        tap_kv = nc.dram_tensor("tap_kv", [2 * D, TC], BF16, kind="ExternalOutput")
        tap_o = nc.dram_tensor("tap_o", [D, TC], BF16, kind="ExternalOutput")
        tap_p = nc.dram_tensor("tap_p", [16 * 128, 512], BF16, kind="ExternalOutput")

    exkv = nc.dram_tensor("exkv", [2 * D, TC], BF16, kind="Internal")
    exk = exkv.ap()[0:D]
    exv = exkv.ap()[D:2 * D]
    exn = nc.dram_tensor("exn", [1, TC], F32, kind="Internal")
    exkvg = nc.dram_tensor("exkvg", [4 * D, TC], BF16, kind="Internal")
    exng = nc.dram_tensor("exng", [2, TC], F32, kind="Internal")
    qstash = nc.dram_tensor("qstash", [D, TC], BF16, kind="Internal")

    ctx = ExitStack()
    sb = lambda n_, s_, d_: ctx.enter_context(nc.sbuf_tensor(n_, s_, d_))
    psa = lambda n_, s_, d_: ctx.enter_context(nc.psum_tensor(n_, s_, d_))

    X = sb("X", [128, NFC * TC], F32)
    XB = sb("XB", [128, NFC * TC], BF16)
    MSK = sb("MSK", [128, 8 * 512], BF16)
    BIASV = sb("BIASV", [128, 64], F32)
    KH = [sb(f"KH{i}", [128, 2 * TC], BF16) for i in range(2)]
    VH = [sb(f"VH{i}", [128, 2 * TC], BF16) for i in range(2)]
    P = [sb(f"Pt{i}", [128, 512], BF16) for i in range(3)]
    WR = [sb(f"WRs{i}", [128, NFC * 128], BF16) for i in range(NSLOT)]
    BC = sb("BC", [128, 2 * TC], F32)
    STAT = sb("STAT", [128, 3 * TC], F32)
    RT = sb("RT", [128, TC], BF16)
    RTS = sb("RTS", [128, TC], BF16)
    RSC = sb("RSC", [128, TC], BF16)
    HT = [sb("HTt0", [128, 8 * TC], BF16)] * 2
    SIL = sb("SIL", [128, TC], BF16)
    SCK = [sb(f"SCKt{i}", [128, TC], BF16) for i in range(2)]
    SCV = sb("SCV", [128, TC], BF16)
    SQB = sb("SQB", [128, TC], BF16)
    QS = [sb(f"QSt{i}", [128, 512], BF16) for i in range(2)]
    QT = [sb(f"QTt{i}", [128, 512], BF16) for i in range(2)]
    RRK = sb("RRK", [128, 16], F32)
    LOG = sb("LOG", [128, 64], F32)
    DW = sb("DW", [128, 64], F32)
    TPR = sb("TPR", [128, 32], F32)
    WRS = sb("WRS", [128, NFC * 8], BF16)
    QNW = sb("QNW", [128, 16], F32)
    KNW = sb("KNW", [128, 16], F32)
    IDN = sb("IDN", [128, 128], F32)
    IDNB = sb("IDNB", [128, 128], BF16)
    ONE = sb("ONE", [128, 1], F32)
    ONEB = sb("ONEB", [128, 1], BF16)
    ONER = sb("ONER", [128, 128], F32)
    CST = sb("CST", [128, 3], F32)

    pb = [psa(f"pb{i}", [128, 512], F32) if i != 5 else
          psa("pb5b", [128, 1024], BF16) for i in range(8)]
    pbt = pb[5]

    sIN = kb.sem("sIN")
    sST = kb.sem("sST")
    sMSK = kb.sem("sMSK")
    sRING = [kb.sem(f"sRING{i}") for i in range(NSLOT)]
    sKH = [kb.sem(f"sKH{i}") for i in range(2)]
    sVH = [kb.sem(f"sVH{i}") for i in range(2)]
    sQSm = [kb.sem(f"sQS{i}") for i in range(2)]
    sSCK = [kb.sem(f"sSCK{i}") for i in range(2)]
    sSCV = kb.sem("sSCV")
    sEXN = kb.sem("sEXN")
    sMM = kb.sem("sMM")
    sEV = kb.sem("sEV")
    kb._sev = sEV
    sAC = kb.sem("sAC")
    kb._sac = sAC
    sGP = kb.sem("sGP")
    sCC = [kb.sem(f"sCC{i}") for i in range(4)]
    sAUX = kb.sem("sAUX")

    wseq = []
    wseq += [(wqkv_a.ap()[j], 16) for j in range(48)]
    wseq += [(wout_a.ap()[j], 16) for j in range(16)]
    ffn_seq = []
    for e in range(8):
        for ic in range(8):
            ffn_seq.append((wgu.ap()[e, ic, 0], 16))
            ffn_seq.append((wgu.ap()[e, ic, 1], 16))
        for of in range(16):
            ffn_seq.append((wdn.ap()[e, of], 8))
    FB1 = len(wseq)
    wseq += ffn_seq
    QB2 = len(wseq)
    wseq += [(wqkv_b.ap()[j], 16) for j in range(48)]
    wseq += [(wo_b.ap()[j], 16) for j in range(16)]
    FB2 = len(wseq)
    wseq += ffn_seq

    ring = {"emitted": 0, "released": [None] * len(wseq)}

    def ring_ensure(i):
        while ring["emitted"] <= min(i + PREF, len(wseq) - 1):
            j = ring["emitted"]
            src, nkc = wseq[j]
            slot = WR[j % NSLOT]
            waits = []
            if j >= NSLOT:
                rel = ring["released"][j - NSLOT]
                assert rel is not None
                waits.append(rel)
            src3 = src.rearrange("(kc p) o -> p kc o", p=128)
            dst3 = slot[:, 0:nkc * 128].rearrange("p (kc o) -> p kc o", kc=nkc)
            kb.gp(lambda e, d=dst3, s=src3: e.dma_start(out=d, in_=s),
                  waits=waits, inc=sRING[j % NSLOT])
            ring["emitted"] += 1

    def ring_use(i):
        ring_ensure(i)
        nuse = i // NSLOT + 1
        return WR[i % NSLOT], (sRING[i % NSLOT], 16 * nuse)

    def ring_release(i):
        ring["released"][i] = (sMM, sMM.n)

    for fc in range(NFC):
        kb.io(lambda e, fc=fc: e.dma_start(out=X[:, fc * TC:(fc + 1) * TC],
                                           in_=x0[fc * 128:(fc + 1) * 128, :]), inc=sIN)
    kb.io(lambda e: e.dma_start(out=IDN[:, :], in_=ident[:, :]), inc=sIN)
    kb.io(lambda e: e.dma_start(out=QNW[:, :], in_=qnw_d[:, :]), inc=sIN)
    kb.io(lambda e: e.dma_start(out=KNW[:, :], in_=knw_d[:, :]), inc=sIN)
    kb.io(lambda e: e.dma_start(out=BIASV[:, :], in_=biasv_d[:, :]), inc=sIN)
    kb.gp(lambda e: e.dma_start(out=RT[:, :], in_=ropec_d[:, :]), inc=sAUX)
    kb.gp(lambda e: e.dma_start(out=RTS[:, :], in_=ropes_d[:, :]), inc=sAUX)
    kb.gp(lambda e: e.dma_start(
        out=WRS[:, :].rearrange("p (kc o) -> p kc o", kc=NFC),
        in_=wrt.ap().rearrange("(kc p) o -> p kc o", p=128)), inc=sAUX)
    kb.gp(lambda e: e.memset(ONE[:, :], 1.0), inc=sGP, incval=1)
    kb.gp(lambda e: e.memset(ONEB[:, :], 1.0), inc=sGP, incval=1)
    kb.gp(lambda e: e.memset(ONER[:, :], 1.0), inc=sGP, incval=1)
    kb.gp(lambda e: e.memset(CST[:, 0:1], SHIFT), inc=sGP, incval=1)
    kb.gp(lambda e: e.memset(CST[:, 1:2], 1e-5), inc=sGP, incval=1)
    kb.gp(lambda e: e.memset(CST[:, 2:3], 1e-6), inc=sGP, incval=1)
    pre_in = (sIN, sIN.n)
    pre_gp = (sGP, sGP.n)
    pre_aux = (sAUX, sAUX.n)
    kb.dve(lambda e: e.tensor_copy(IDNB[:, :], IDN[:, :]), waits=[pre_in, pre_gp],
           inc=sEV)

    bankfree = [(sEV, 0)] * 8
    sck_rel = [(sEV, 0), (sEV, 0)]
    kh_rel = [(sEV, 0), (sEV, 0)]
    vh_rel = [(sEV, 0), (sEV, 0)]
    qs_rel = [(sEV, 0), (sEV, 0)]
    qt_rel = [(sEV, 0), (sEV, 0)]
    ht_rel = [(sEV, 0), (sEV, 0)]
    msk_rel = [(sEV, 0)]
    scv_rel = [(sEV, 0)]

    def emit_ln(eps=1e-5):
        x_ready = (sEV, sEV.n)
        for tc in range(2):
            for fc in range(NFC):
                xs = X[:, fc * TC + tc * 512: fc * TC + tc * 512 + 512]
                w = [x_ready, pre_gp, bankfree[3 + tc]] if fc == 0 else ()
                kb.pe(lambda e, xs=xs, tc=tc, fc=fc: e.matmul(
                    pb[3 + tc][0:1, :], ONE[:, :], xs,
                    start=(fc == 0), stop=(fc == NFC - 1)),
                    waits=w, inc=sMM if fc == NFC - 1 else None)
        sq_rel = {0: (sMM, 0), 1: (sMM, 0)}
        for tc in range(2):
            for fc in range(NFC):
                xs = X[:, fc * TC + tc * 512: fc * TC + tc * 512 + 512]
                sq = SQB[:, (fc % 2) * 512:(fc % 2) * 512 + 512]
                wsq = [sq_rel[fc % 2]]
                if tc == 0 and fc == 0:
                    wsq.append(x_ready)
                kb.act(lambda e, sq=sq, xs=xs: e.activation(sq, xs, AF.Square),
                       waits=wsq, inc=sAC)
                w = [(sAC, sAC.n)]
                if fc == 0:
                    w.append(bankfree[6 + tc])
                kb.pe(lambda e, sq=sq, tc=tc, fc=fc: e.matmul(
                    pb[6 + tc][0:1, :], ONEB[:, :], sq,
                    start=(fc == 0), stop=(fc == NFC - 1)),
                    waits=w, inc=sMM)
                sq_rel[fc % 2] = (sMM, sMM.n)
        mm_done = (sMM, sMM.n)
        for tc in range(2):
            sl = slice(tc * 512, tc * 512 + 512)
            sl1 = slice(TC + tc * 512, TC + tc * 512 + 512)
            kb.dve(lambda e, sl=sl, tc=tc: e.tensor_scalar(
                STAT[0:1, sl], pb[3 + tc][0:1, :], 1.0 / D, None, ALU.mult),
                waits=[mm_done], inc=sEV)
            kb.dve(lambda e, sl1=sl1, tc=tc: e.tensor_scalar(
                STAT[0:1, sl1], pb[6 + tc][0:1, :], 1.0 / D, None, ALU.mult),
                inc=sEV)
        kb.dve(lambda e: e.engine_nop(), waits=[(sEV, sEV.n)])
        kb.dve(lambda e: e.tensor_tensor(
            STAT[0:1, 2 * TC:3 * TC], STAT[0:1, 0:TC], STAT[0:1, 0:TC],
            ALU.mult), inc=sEV)
        kb.dve(lambda e: e.tensor_tensor(
            STAT[0:1, TC:2 * TC], STAT[0:1, TC:2 * TC],
            STAT[0:1, 2 * TC:3 * TC], ALU.subtract), inc=sEV)
        for i in (3, 4, 6, 7):
            bankfree[i] = (sEV, sEV.n)
        kb.act(lambda e: e.activation(STAT[0:1, 2 * TC:3 * TC],
                                      STAT[0:1, TC:2 * TC], AF.Ln,
                                      bias=CST[0:1, 1:2]),
               waits=[(sEV, sEV.n), pre_gp], inc=sAC)
        kb.act(lambda e: e.activation(STAT[0:1, 2 * TC:3 * TC],
                                      STAT[0:1, 2 * TC:3 * TC], AF.Exp,
                                      scale=-0.5), inc=sAC)
        inv_done = (sAC, sAC.n)
        for tc in range(2):
            sl = slice(tc * 512, tc * 512 + 512)
            for (so, do) in ((0, 0), (2 * TC, TC)):
                kb.pe(lambda e, sl=sl, so=so, tc=tc: e.matmul(
                    pb[4][:, :], ONER[0:1, :],
                    STAT[0:1, so + tc * 512: so + tc * 512 + 512],
                    start=True, stop=True),
                    waits=[inv_done, bankfree[4]], inc=sMM)
                kb.dve(lambda e, sl=sl, do=do, tc=tc: e.tensor_copy(
                    BC[:, do + tc * 512: do + tc * 512 + 512], pb[4][:, :]),
                    waits=[(sMM, sMM.n)], inc=sEV)
                bankfree[4] = (sEV, sEV.n)
        for fc in range(NFC):
            xs = X[:, fc * TC:(fc + 1) * TC]
            xb = XB[:, fc * TC:(fc + 1) * TC]
            kb.dve(lambda e, xb=xb, xs=xs: e.tensor_tensor(
                xb, xs, BC[:, 0:TC], ALU.subtract), inc=sEV)
            kb.dve(lambda e, xb=xb: e.tensor_tensor(
                xb, xb, BC[:, TC:2 * TC], ALU.mult), inc=sEV)
        return (sEV, sEV.n)

    nbk = {"i": 0}

    def two_banks():
        i = nbk["i"] % 2
        nbk["i"] += 1
        return [2 * i, 2 * i + 1]

    def proj_block(block_i, ln_done, nkc=16):
        slot, wv = ring_use(block_i)
        bks = two_banks()
        out = []
        for tc in range(2):
            b = bks[tc]
            for kc in range(nkc):
                lhs = slot[:, kc * 128:(kc + 1) * 128]
                rhs = XB[:, kc * TC + tc * 512: kc * TC + tc * 512 + 512]
                w = [wv, ln_done, bankfree[b]] if kc == 0 else ()
                kb.pe(lambda e, b=b, lhs=lhs, rhs=rhs, kc=kc, nkc=nkc: e.matmul(
                    pb[b][:, :], lhs, rhs, start=(kc == 0), stop=(kc == nkc - 1)),
                    waits=w, inc=sMM if kc == nkc - 1 else None)
            out.append((b, (sMM, sMM.n)))
        ring_release(block_i)
        return out

    def rope_chunk(buf, c0, T, rt_off):
        x1, x2 = buf[0:64, c0:c0 + T], buf[64:128, c0:c0 + T]
        s1, s2 = RSC[0:64, 0:T], RSC[64:128, 0:T]
        sl = slice(rt_off, rt_off + T)
        kb.dve(lambda e: e.tensor_tensor(s1, x2, RTS[64:128, sl], ALU.mult),
               waits=[pre_aux], inc=sEV)
        kb.dve(lambda e: e.tensor_tensor(s2, x1, RTS[0:64, sl], ALU.mult), inc=sEV)
        kb.dve(lambda e: e.tensor_tensor(x1, x1, RT[0:64, sl], ALU.mult), inc=sEV)
        kb.dve(lambda e: e.tensor_tensor(x2, x2, RT[64:128, sl], ALU.mult), inc=sEV)
        kb.dve(lambda e: e.tensor_tensor(x1, x1, s1, ALU.subtract), inc=sEV)
        kb.dve(lambda e: e.tensor_tensor(x2, x2, s2, ALU.add), inc=sEV)

    def emit_qkv(layer, base, ln_done, exchange_fn=None):
        isb = (layer == 1)
        sqkv_rel = {0: (sMM, 0), 1: (sMM, 0)}
        for h in range(H):
            res = proj_block(base + h, ln_done)
            sc = SCK[h % 2]
            for tc in range(2):
                b, mk = res[tc]
                kb.dve(lambda e, sc=sc, tc=tc, b=b: e.tensor_copy(
                    sc[:, tc * 512:tc * 512 + 512], pb[b][:, :]),
                    waits=[mk, sck_rel[h % 2]] if tc == 0 else [mk], inc=sEV)
                bankfree[b] = (sEV, sEV.n)
            if isb:
                for tc in range(2):
                    sl = slice(tc * 512, tc * 512 + 512)
                    kb.act(lambda e, sc=sc, sl=sl: e.activation(
                        SQB[:, sl], sc[:, sl], AF.Square),
                        waits=[(sEV, sEV.n), sqkv_rel[tc]], inc=sAC)
                    w = [(sAC, sAC.n)]
                    if h == 0:
                        w.append(bankfree[6 + tc])
                    kb.pe(lambda e, sl=sl, h=h, tc=tc: e.matmul(
                        pb[6 + tc][0:1, :], ONEB[:, :], SQB[:, sl],
                        start=(h == 0), stop=(h == H - 1)),
                        waits=w, inc=sMM)
                    sqkv_rel[tc] = (sMM, sMM.n)
                kb.dve(lambda e, sc=sc, h=h: e.tensor_scalar(
                    sc[:, :], sc[:, :], KNW[:, h:h + 1], None, ALU.mult),
                    waits=[(sAC, sAC.n)], inc=sEV)
                rope_chunk(sc, 0, TC, 0)
            w = [(sEV, sEV.n)]
            if layer == 1 and h == 0:
                w.append((sCC[0], 1))
            kb.io(lambda e, h=h, sc=sc: e.dma_start(
                out=exk[h * 128:(h + 1) * 128, :], in_=sc[:, :]),
                waits=w, inc=sSCK[h % 2])
            sck_rel[h % 2] = (sSCK[h % 2], sSCK[h % 2].n)
        if isb:
            ss_done = (sMM, sMM.n)
            for tc in range(2):
                sl = slice(tc * 512, tc * 512 + 512)
                kb.dve(lambda e, sl=sl, tc=tc: e.tensor_scalar(
                    STAT[0:1, sl], pb[6 + tc][0:1, :], 1.0 / D, None, ALU.mult),
                    waits=[ss_done, (sEXN, sEXN.n)], inc=sEV)
                bankfree[6 + tc] = (sEV, sEV.n)
            kb.act(lambda e: e.activation(STAT[0:1, 0:TC], STAT[0:1, 0:TC], AF.Ln,
                                          bias=CST[0:1, 2:3]),
                   waits=[(sEV, sEV.n)], inc=sAC)
            kb.act(lambda e: e.activation(STAT[0:1, 0:TC], STAT[0:1, 0:TC], AF.Exp,
                                          scale=-0.5), inc=sAC)
            kb.io(lambda e: e.dma_start(out=exn[0:1, :], in_=STAT[0:1, 0:TC]),
                  waits=[(sAC, sAC.n), (sCC[0], 1)], inc=sEXN)
        for h in range(H):
            res = proj_block(base + 16 + h, ln_done)
            sc = SCK[h % 2]
            for tc in range(2):
                b, mk = res[tc]
                kb.dve(lambda e, sc=sc, tc=tc, b=b: e.tensor_copy(
                    sc[:, tc * 512:tc * 512 + 512], pb[b][:, :]),
                    waits=[mk, sck_rel[h % 2]] if tc == 0 else [mk], inc=sEV)
                bankfree[b] = (sEV, sEV.n)
            evv = (sEV, sEV.n)
            for tt in range(8):
                w = [evv] if tt == 0 else ()
                if tt == 0:
                    w = [evv, bankfree[5]]
                kb.pe(lambda e, sc=sc, tt=tt: e.transpose(
                    pbt[:, tt * 128:(tt + 1) * 128],
                    sc[:, tt * 128:(tt + 1) * 128], IDNB[:, :]),
                    waits=w, inc=sMM if tt == 7 else None)
            tr_done = (sMM, sMM.n)
            sck_rel[h % 2] = tr_done
            kb.dve(lambda e: e.tensor_copy(SCV[:, :], pbt[:, :]),
                   waits=[tr_done, scv_rel[0]], inc=sEV)
            bankfree[5] = (sEV, sEV.n)
            w = [(sEV, sEV.n)]
            if layer == 1 and h == 0:
                w.append((sCC[0], 1))
            kb.io(lambda e, h=h: e.dma_start(
                out=exv[h * 128:(h + 1) * 128, :], in_=SCV[:, :]),
                waits=w, inc=sSCV)
            scv_rel[0] = (sSCV, sSCV.n)
        if exchange_fn is not None:
            exchange_fn()
        for h in range(H):
            res = proj_block(base + 32 + h, ln_done)
            sc = SCK[h % 2]
            for tc in range(2):
                b, mk = res[tc]
                kb.dve(lambda e, sc=sc, tc=tc, b=b: e.tensor_copy(
                    sc[:, tc * 512:tc * 512 + 512], pb[b][:, :]),
                    waits=[mk, sck_rel[h % 2]] if tc == 0 else [mk], inc=sEV)
                bankfree[b] = (sEV, sEV.n)
            if isb:
                for tc in range(2):
                    sl = slice(tc * 512, tc * 512 + 512)
                    kb.act(lambda e, sc=sc, sl=sl: e.activation(
                        SQB[:, sl], sc[:, sl], AF.Square),
                        waits=[(sEV, sEV.n), sqkv_rel[tc]], inc=sAC)
                    w = [(sAC, sAC.n)]
                    if h == 0:
                        w.append(bankfree[6 + tc])
                    kb.pe(lambda e, sl=sl, h=h, tc=tc: e.matmul(
                        pb[6 + tc][0:1, :], ONEB[:, :], SQB[:, sl],
                        start=(h == 0), stop=(h == H - 1)),
                        waits=w, inc=sMM)
                    sqkv_rel[tc] = (sMM, sMM.n)
            w = [(sEV, sEV.n)]
            if layer == 1 and h == 0:
                w.append((sQSm[0], sQSm[0].n))
                w.append((sQSm[1], sQSm[1].n))
            kb.io(lambda e, h=h, sc=sc: e.dma_start(
                out=qstash[h * 128:(h + 1) * 128, :], in_=sc[:, :]),
                waits=w, inc=sSCK[h % 2])
            sck_rel[h % 2] = (sSCK[h % 2], sSCK[h % 2].n)
        if isb:
            ss_done = (sMM, sMM.n)
            for tc in range(2):
                sl1 = slice(TC + tc * 512, TC + tc * 512 + 512)
                kb.dve(lambda e, sl1=sl1, tc=tc: e.tensor_scalar(
                    STAT[0:1, sl1], pb[6 + tc][0:1, :], 1.0 / D, None, ALU.mult),
                    waits=[ss_done], inc=sEV)
                bankfree[6 + tc] = (sEV, sEV.n)
            kb.act(lambda e: e.activation(STAT[0:1, TC:2 * TC],
                                          STAT[0:1, TC:2 * TC],
                                          AF.Ln, bias=CST[0:1, 2:3]),
                   waits=[(sEV, sEV.n)], inc=sAC)
            kb.act(lambda e: e.activation(STAT[0:1, TC:2 * TC],
                                          STAT[0:1, TC:2 * TC],
                                          AF.Exp, scale=-0.5), inc=sAC)
            kb.dve(lambda e: e.tensor_scalar(STAT[0:1, TC:2 * TC],
                                             STAT[0:1, TC:2 * TC],
                                             RQ, None, ALU.mult),
                   waits=[(sAC, sAC.n)], inc=sEV)
            for tc in range(2):
                sl = slice(tc * 512, tc * 512 + 512)
                kb.pe(lambda e, tc=tc: e.matmul(
                    pb[4][:, :], ONER[0:1, :],
                    STAT[0:1, TC + tc * 512: TC + tc * 512 + 512],
                    start=True, stop=True),
                    waits=[(sEV, sEV.n), bankfree[4]], inc=sMM)
                kb.dve(lambda e, sl=sl: e.tensor_copy(BC[:, sl], pb[4][:, :]),
                       waits=[(sMM, sMM.n)], inc=sEV)
                bankfree[4] = (sEV, sEV.n)
        return [(sSCK[0], sSCK[0].n), (sSCK[1], sSCK[1].n)]

    def emit_exchange(idx, with_n=False):
        waits = [(sSCK[0], sSCK[0].n), (sSCK[1], sSCK[1].n), (sSCV, sSCV.n)]
        if idx >= 1:
            waits += [(sKH[0], sKH[0].n), (sKH[1], sKH[1].n),
                      (sVH[0], sVH[0].n), (sVH[1], sVH[1].n),
                      (sIN, sIN.n)]
        kb.gp(lambda e: e.collective_compute(
            "AllGather", ALU.bypass,
            replica_groups=[[0, 1], [2, 3], [4, 5], [6, 7]],
            ins=[exkv.ap().opt()],
            outs=[exkvg.ap().opt()]),
            waits=waits, inc=sCC[idx], incval="default")
        if with_n:
            kb.gp(lambda e: e.collective_compute(
                "AllGather", ALU.bypass,
                replica_groups=[[0, 1], [2, 3], [4, 5], [6, 7]],
                ins=[exn.ap().opt()],
                outs=[exng.ap().opt()]),
                waits=[(sEXN, sEXN.n)], inc=sCC[idx + 1], incval="default")

    def emit_attn(layer, maskt, cc_idx, stash_done):
        isb = (layer == 1)
        ccw = (sCC[cc_idx], 1)
        if isb:
            kb.io(lambda e: e.dma_start(out=STAT[0:1, 0:TC], in_=exng[0:1, :]),
                  waits=[ccw, (sCC[cc_idx + 1], 1), (sEXN, sEXN.n),
                         (sMM, sMM.n), (sEV, sEV.n), (sAC, sAC.n)], inc=sIN)
            kb.io(lambda e: e.dma_start(out=STAT[0:1, TC:2 * TC],
                                        in_=exng[1:2, :]), inc=sIN)
            rr_in = (sIN, sIN.n)
            for kt in range(16):
                w = [rr_in, pre_in]
                if kt == 0:
                    w.append(bankfree[4])
                src = STAT[0:1, kt * 128:(kt + 1) * 128]
                kb.pe(lambda e, kt=kt, src=src: e.transpose(
                    pb[4][:, kt:kt + 1], src, IDN[0:1, 0:1]),
                    waits=w, inc=sMM if kt == 15 else None)
            kb.dve(lambda e: e.tensor_copy(RRK[:, :], pb[4][:, 0:16]),
                   waits=[(sMM, sMM.n)], inc=sEV)
            bankfree[4] = (sEV, sEV.n)

        def reload(qc, h):
            hh = h % 2
            ksrc = exkvg.ap().rearrange("(r x) t -> x r t", r=2)[
                h * 128:(h + 1) * 128]
            kdst = KH[hh][:, :].rearrange("p (r t) -> p r t", r=2)
            kb.io(lambda e, kdst=kdst, ksrc=ksrc: e.dma_start(out=kdst, in_=ksrc),
                  waits=[ccw, kh_rel[hh]], inc=sKH[hh])
            vsrc = exkvg.ap().rearrange("(r x) t -> x r t", r=2)[
                D + h * 128:D + (h + 1) * 128]
            vdst = VH[hh][:, :].rearrange("p (r t) -> p r t", r=2)
            kb.io(lambda e, vdst=vdst, vsrc=vsrc: e.dma_start(out=vdst, in_=vsrc),
                  waits=[ccw, vh_rel[hh]], inc=sVH[hh])
            kb.io(lambda e, h=h, qc=qc, hh=hh: e.dma_start(
                out=QS[hh][:, :],
                in_=qstash[h * 128:(h + 1) * 128, qc * 512:qc * 512 + 512]),
                waits=[qs_rel[hh]] + stash_done, inc=sQSm[hh])
            return (sKH[hh], sKH[hh].n), (sVH[hh], sVH[hh].n), (sQSm[hh], sQSm[hh].n)

        exp_free = [bankfree[0], bankfree[1], bankfree[2]]
        p_extra = {}
        pend = {}
        pending_norm = [None]
        pend_av = []

        def bdset(qc):
            return list(range(4 * qc, 4 * qc + 4)) + \
                list(range(4 * qc + 8, 4 * qc + 12))

        for qc in range(2):
            ktmax = 12 if qc == 0 else 16
            bset = bdset(qc)
            bidx = {kt: j for j, kt in enumerate(bset)}
            mw = [msk_rel[0]]
            for j in range(8):
                kb.io(lambda e, j=j, qc=qc: e.dma_start(
                    out=MSK[:, j * 512:(j + 1) * 512],
                    in_=maskadd.ap()[layer, qc, j]),
                    waits=mw if j == 0 else (), inc=sMSK)
                mw = ()
            msk_w = (sMSK, sMSK.n)
            if qc == 0:
                pend[0] = reload(0, 0)
            for h in range(H):
                hh = h % 2
                khw, vhw, qsw = pend[h]
                if isb:
                    kb.dve(lambda e, hh=hh, qc=qc: e.tensor_tensor(
                        QT[hh][:, :], QS[hh][:, :],
                        BC[:, qc * 512:qc * 512 + 512], ALU.mult),
                        waits=[qsw, qt_rel[hh]], inc=sEV)
                    kb.dve(lambda e, hh=hh, h=h: e.tensor_scalar(
                        QT[hh][:, :], QT[hh][:, :], QNW[:, h:h + 1], None,
                        ALU.mult), inc=sEV)
                    rope_chunk(QT[hh], 0, 512, qc * 512)
                    qs_rel[hh] = (sEV, sEV.n)
                    qrhs = QT[hh]
                else:
                    qrhs = QS[hh]
                avb = 3 if h % 2 == 0 else 4
                smb = 6 if h % 2 == 0 else 7
                avbox = {}
                for kt in range(ktmax):
                    if kt == 4:
                        if h + 1 < H:
                            pend[h + 1] = reload(qc, h + 1)
                        elif qc == 0:
                            pend[0] = reload(1, 0)
                            pend.pop(1, None)
                    sbk = kt % 3
                    w = [exp_free[sbk]]
                    if kt < 3:
                        w.append(khw)
                        if not isb:
                            w.append(qsw)
                        else:
                            w.append((sEV, sEV.n))
                    isbd = kt in bidx
                    kb.pe(lambda e, sbk=sbk, kt=kt, qrhs=qrhs, hh=hh,
                          isbd=isbd: e.matmul(
                              pb[sbk][:, :], KH[hh][:, kt * 128:(kt + 1) * 128],
                              qrhs[:, :], start=True, stop=not isbd),
                          waits=w, inc=None if isbd else sMM)
                    if isbd:
                        j = bidx[kt]
                        kb.pe(lambda e, sbk=sbk, j=j: e.matmul(
                            pb[sbk][:, :], IDNB[:, :],
                            MSK[:, j * 512:(j + 1) * 512],
                            start=False, stop=True),
                            waits=[msk_w], inc=sMM)
                    scl = RRK[:, kt:kt + 1] if isb else RQ
                    bia = BIASV[:, layer * 32 + qc * 16 + kt:
                                layer * 32 + qc * 16 + kt + 1]
                    wexp = [(sMM, sMM.n), p_free[sbk]]
                    kb.act(lambda e, sbk=sbk, scl=scl, bia=bia: e.activation(
                        P[sbk][:, :], pb[sbk][:, :], AF.Exp,
                        bias=bia, scale=scl),
                        waits=wexp, inc=sAC)
                    exp_free[sbk] = (sAC, sAC.n)
                    pmask = (sAC, sAC.n)

                    def mk_av(sbk=sbk, kt=kt, hh=hh, avb=avb, smb=smb,
                              ktmax=ktmax, pmask=pmask, vhw=vhw, box=avbox,
                              isb=isb):
                        def av():
                            w2 = [pmask]
                            if kt == 0:
                                w2 += [vhw, bankfree[avb], bankfree[smb]]
                            kb.pe(lambda e: e.matmul(
                                pb[avb][:, :],
                                VH[hh][:, kt * 128:(kt + 1) * 128],
                                P[sbk][:, :], start=(kt == 0),
                                stop=(kt == ktmax - 1)),
                                waits=w2)
                            kb.pe(lambda e: e.matmul(
                                pb[smb][0:1, :], ONEB[:, :], P[sbk][:, :],
                                start=(kt == 0), stop=(kt == ktmax - 1)),
                                inc=sMM)
                            p_free[sbk] = (sMM, sMM.n)
                            if kt == ktmax - 1:
                                box["done"] = (sMM, sMM.n)
                                kh_rel[hh] = box["done"]
                                vh_rel[hh] = box["done"]
                                if not isb:
                                    qs_rel[hh] = box["done"]
                                else:
                                    qt_rel[hh] = box["done"]
                        return av
                    pend_av.append(mk_av())
                    if len(pend_av) > 2:
                        pend_av.pop(0)()
                if pending_norm[0] is not None:
                    pending_norm[0]()

                def mk_norm(h=h, qc=qc, avb=avb, smb=smb, box=avbox):
                    def norm():
                        av_done = box["done"]
                        kb.dve(lambda e: e.reciprocal(
                            STAT[0:1, 2 * TC:2 * TC + 512], pb[smb][0:1, :]),
                            waits=[av_done], inc=sEV)
                        bankfree[smb] = (sEV, sEV.n)
                        kb.pe(lambda e: e.matmul(
                            pb[smb][:, :], ONER[0:1, :],
                            STAT[0:1, 2 * TC:2 * TC + 512],
                            start=True, stop=True),
                            waits=[(sEV, sEV.n)], inc=sMM)
                        kb.dve(lambda e: e.tensor_copy(
                            BC[:, 3 * 512:4 * 512], pb[smb][:, :]),
                            waits=[(sMM, sMM.n)], inc=sEV)
                        bankfree[smb] = (sEV, sEV.n)
                        kb.dve(lambda e: e.tensor_tensor(
                            XB[:, h * TC + qc * 512: h * TC + qc * 512 + 512],
                            pb[avb][:, :], BC[:, 3 * 512:4 * 512], ALU.mult),
                            waits=[av_done], inc=sEV)
                        bankfree[avb] = (sEV, sEV.n)
                    return norm
                pending_norm[0] = mk_norm()
            while pend_av:
                pend_av.pop(0)()
            pending_norm[0]()
            pending_norm[0] = None
            msk_rel[0] = (sMM, sMM.n)
        for i in range(3):
            bankfree[i] = exp_free[i]

    def emit_outproj(base):
        o_ready = (sEV, sEV.n)
        for of in range(NFC):
            res = proj_block(base + of, o_ready)
            for tc in range(2):
                b, mk = res[tc]
                xsl = X[:, of * TC + tc * 512: of * TC + tc * 512 + 512]
                kb.dve(lambda e, xsl=xsl, b=b: e.tensor_tensor(
                    xsl, pb[b][:, :], xsl, ALU.add), waits=[mk], inc=sEV)
                bankfree[b] = (sEV, sEV.n)

    def emit_ffn(base, ln_done):
        sil_rel = [(sEV, 0), (sEV, 0)]
        for tt in range(8):
            b = 3 + tt % 2
            for fc in range(NFC):
                w = []
                if fc == 0:
                    w = [ln_done, pre_aux, bankfree[b]]
                kb.pe(lambda e, b=b, fc=fc, tt=tt: e.matmul(
                    pb[b][0:128, 0:8],
                    XB[:, fc * TC + tt * 128: fc * TC + tt * 128 + 128],
                    WRS[:, fc * 8:(fc + 1) * 8],
                    start=(fc == 0), stop=(fc == NFC - 1)),
                    waits=w, inc=sMM if fc == NFC - 1 else None)
            kb.dve(lambda e, b=b, tt=tt: e.tensor_copy(
                LOG[:, tt * 8:(tt + 1) * 8], pb[b][0:128, 0:8]),
                waits=[(sMM, sMM.n)], inc=sEV)
            bankfree[b] = (sEV, sEV.n)
        def dves(fn):
            kb.dve(fn, waits=[(sEV, sEV.n)], inc=sEV)
        for tt in range(8):
            lg = LOG[:, tt * 8:(tt + 1) * 8]
            dves(lambda e, lg=lg: e.reduce_max(TPR[:, 0:1], lg, axis=AX.X))
            dves(lambda e, lg=lg: e.tensor_scalar(
                TPR[:, 4:12], lg, TPR[:, 0:1], None, ALU.is_equal))
            dves(lambda e: e.tensor_scalar(
                TPR[:, 20:28], TPR[:, 4:12], -1e30, None, ALU.mult))
            dves(lambda e, lg=lg: e.tensor_tensor(
                TPR[:, 20:28], TPR[:, 20:28], lg, ALU.add))
            dves(lambda e: e.reduce_max(TPR[:, 1:2], TPR[:, 20:28], axis=AX.X))
            dves(lambda e, lg=lg: e.tensor_scalar(
                TPR[:, 12:20], lg, TPR[:, 1:2], None, ALU.is_equal))
            dves(lambda e: e.tensor_tensor(
                TPR[:, 2:3], TPR[:, 1:2], TPR[:, 0:1], ALU.subtract))
            kb.act(lambda e: e.activation(TPR[:, 2:3], TPR[:, 2:3], AF.Exp),
                   waits=[(sEV, sEV.n)], inc=sAC)
            kb.dve(lambda e: e.tensor_scalar(
                TPR[:, 28:29], TPR[:, 2:3], 1.0, None, ALU.add),
                waits=[(sAC, sAC.n)], inc=sEV)
            dves(lambda e: e.reciprocal(TPR[:, 2:3], TPR[:, 28:29]))
            dves(lambda e: e.tensor_scalar(
                TPR[:, 3:4], TPR[:, 2:3], -1.0, 1.0, ALU.mult, ALU.add))
            dves(lambda e: e.tensor_scalar(
                TPR[:, 4:12], TPR[:, 4:12], TPR[:, 2:3], None, ALU.mult))
            dves(lambda e: e.tensor_scalar(
                TPR[:, 12:20], TPR[:, 12:20], TPR[:, 3:4], None, ALU.mult))
            dves(lambda e, tt=tt: e.tensor_tensor(
                DW[:, tt * 8:(tt + 1) * 8], TPR[:, 4:12], TPR[:, 12:20], ALU.add))
        dw_done = (sEV, sEV.n)
        if DEBUG_TAPS:
            kb.io(lambda e: e.dma_start(out=tap_log.ap(), in_=LOG[:, :]),
                  waits=[dw_done], inc=sST)
            kb.io(lambda e: e.dma_start(out=tap_dw.ap(), in_=DW[:, :]),
                  waits=[dw_done], inc=sST)
        for e_ in range(8):
            def mk_bw(e_=e_):
                def bw():
                    for tc in range(2):
                        for ttl in range(4):
                            tt = tc * 4 + ttl
                            w = [dw_done, pre_in]
                            if ttl == 0:
                                w.append(bankfree[7])
                            kb.pe(lambda e, tt=tt, ttl=ttl: e.transpose(
                                pb[7][0:1, ttl * 128:(ttl + 1) * 128],
                                DW[:, tt * 8 + e_: tt * 8 + e_ + 1], IDN[:, :]),
                                waits=w, inc=sMM if ttl == 3 else None)
                        kb.dve(lambda e, tc=tc: e.tensor_copy(
                            STAT[0:1, 2 * TC + tc * 512:2 * TC + tc * 512 + 512],
                            pb[7][0:1, 0:512]),
                            waits=[(sMM, sMM.n)], inc=sEV)
                        bankfree[7] = (sEV, sEV.n)
                        kb.pe(lambda e, tc=tc: e.matmul(
                            pb[6][:, :], ONER[0:1, :],
                            STAT[0:1, 2 * TC + tc * 512:
                                 2 * TC + tc * 512 + 512],
                            start=True, stop=True),
                            waits=[(sEV, sEV.n), bankfree[6]], inc=sMM)
                        kb.dve(lambda e, tc=tc: e.tensor_copy(
                            SQB[:, tc * 512:tc * 512 + 512], pb[6][:, :]),
                            waits=[(sMM, sMM.n)], inc=sEV)
                        bankfree[6] = (sEV, sEV.n)
                return bw
            bw_emit = mk_bw()
            bw_done = None
            for ic in range(8):
                gi = base + e_ * 32 + 2 * ic
                slotg, wg = ring_use(gi)
                for tc in range(2):
                    for kc in range(NFC):
                        w = [wg, ln_done, bankfree[tc]] if kc == 0 else ()
                        kb.pe(lambda e, tc=tc, kc=kc, slotg=slotg: e.matmul(
                            pb[tc][:, :], slotg[:, kc * 128:(kc + 1) * 128],
                            XB[:, kc * TC + tc * 512: kc * TC + tc * 512 + 512],
                            start=(kc == 0), stop=(kc == NFC - 1)),
                            waits=w, inc=sMM if kc == NFC - 1 else None)
                    kb.act(lambda e, tc=tc: e.activation(
                        SIL[:, tc * 512:tc * 512 + 512], pb[tc][:, :], AF.Silu),
                        waits=[(sMM, sMM.n), sil_rel[tc]], inc=sAC)
                    bankfree[tc] = (sAC, sAC.n)
                ring_release(gi)
                if bw_done is None:
                    bw_emit()
                    bw_done = (sEV, sEV.n)
                ui = gi + 1
                slotu, wu = ring_use(ui)
                for tc in range(2):
                    for kc in range(NFC):
                        w = [wu, bankfree[2 + tc]] if kc == 0 else ()
                        kb.pe(lambda e, tc=tc, kc=kc, slotu=slotu: e.matmul(
                            pb[2 + tc][:, :], slotu[:, kc * 128:(kc + 1) * 128],
                            XB[:, kc * TC + tc * 512: kc * TC + tc * 512 + 512],
                            start=(kc == 0), stop=(kc == NFC - 1)),
                            waits=w, inc=sMM if kc == NFC - 1 else None)
                    hsl = HT[e_ % 2][:, ic * TC + tc * 512: ic * TC + tc * 512 + 512]
                    w = [(sMM, sMM.n), (sAC, sAC.n)]
                    if ic == 0 and tc == 0:
                        w.append(ht_rel[0])
                        w.append(ht_rel[1])
                    kb.dve(lambda e, hsl=hsl, tc=tc: e.tensor_tensor(
                        hsl, pb[2 + tc][:, :], SIL[:, tc * 512:tc * 512 + 512],
                        ALU.mult), waits=w, inc=sEV)
                    sil_rel[tc] = (sEV, sEV.n)
                    kb.dve(lambda e, hsl=hsl, tc=tc: e.tensor_tensor(
                        hsl, hsl, SQB[:, tc * 512:tc * 512 + 512], ALU.mult),
                        waits=[bw_done], inc=sEV)
                    bankfree[2 + tc] = (sEV, sEV.n)
                ring_release(ui)
            ht_done = (sEV, sEV.n)
            if DEBUG_TAPS and e_ == 0 and base == FB1:
                kb.io(lambda e: e.dma_start(out=tap_ht.ap(), in_=HT[0][:, :]),
                      waits=[ht_done], inc=sST)
            bankfree[5] = ht_done
            for of in range(NFC):
                di = base + e_ * 32 + 16 + of
                slotd, wd = ring_use(di)
                for tc in range(2):
                    bd = 4 if (of * 2 + tc) % 2 == 0 else 7
                    for kc in range(8):
                        w = [wd, ht_done, bankfree[bd]] if kc == 0 else ()
                        kb.pe(lambda e, bd=bd, kc=kc, tc=tc, slotd=slotd,
                              e_=e_: e.matmul(
                                  pb[bd][:, :], slotd[:, kc * 128:(kc + 1) * 128],
                                  HT[e_ % 2][:, kc * TC + tc * 512:
                                             kc * TC + tc * 512 + 512],
                                  start=(kc == 0), stop=(kc == 7)),
                              waits=w, inc=sMM if kc == 7 else None)
                    xsl = X[:, of * TC + tc * 512: of * TC + tc * 512 + 512]
                    kb.dve(lambda e, xsl=xsl, bd=bd: e.tensor_tensor(
                        xsl, pb[bd][:, :], xsl, ALU.add),
                        waits=[(sMM, sMM.n)], inc=sEV)
                    bankfree[bd] = (sEV, sEV.n)
                ring_release(di)
            ht_rel[0] = ht_rel[1] = (sMM, sMM.n)

    p_free = [(sMM, 0), (sMM, 0), (sMM, 0)]

    ln1 = emit_ln()
    stash1 = emit_qkv(0, 0, ln1, exchange_fn=lambda: emit_exchange(0))
    if NSUB >= 1:
        emit_attn(0, None, 0, stash1)
        if DEBUG_TAPS and NSUB == 1:
            ev = (sEV, sEV.n)
            for fc in range(NFC):
                kb.io(lambda e, fc=fc: e.dma_start(
                    out=tap_o[fc * 128:(fc + 1) * 128, :],
                    in_=XB[:, fc * TC:(fc + 1) * TC]),
                    waits=[ev] if fc == 0 else (), inc=sST)
            kb.io(lambda e: e.dma_start(out=tap_q.ap(), in_=qstash.ap()), inc=sST)
            kb.io(lambda e: e.dma_start(out=tap_kv.ap(), in_=exkv.ap()), inc=sST)
        emit_outproj(48)
    if NSUB >= 2:
        ln2 = emit_ln()
        if DEBUG_TAPS:
            for fc in range(NFC):
                kb.io(lambda e, fc=fc: e.dma_start(
                    out=tap_o[fc * 128:(fc + 1) * 128, :],
                    in_=XB[:, fc * TC:(fc + 1) * TC]),
                    waits=[ln2] if fc == 0 else (), inc=sST)
        emit_ffn(FB1, ln2)
    if NSUB >= 3:
        ln3 = emit_ln()
        stash2 = emit_qkv(1, QB2, ln3,
                          exchange_fn=lambda: emit_exchange(1, with_n=True))
        emit_attn(1, None, 1, stash2)
        emit_outproj(QB2 + 48)
    if NSUB >= 4:
        ln4 = emit_ln()
        emit_ffn(FB2, ln4)

    fin = (sEV, sEV.n)
    for fc in range(NFC):
        kb.io(lambda e, fc=fc: e.dma_start(out=y[fc * 128:(fc + 1) * 128, :],
                                           in_=X[:, fc * TC:(fc + 1) * TC]),
              waits=[fin] if fc == 0 else (), inc=sST)
    kb.emit("sync", lambda e: None, waits=[(sST, sST.n)])

    kb.check()

    for s in kb.sems:
        s.h = ctx.enter_context(nc.semaphore(s.name))
    with nc.Block() as block:
        for eng in KB.ENGINES:
            getattr(block, eng)(
                (lambda ops: lambda e: [f(e) for f in ops])(kb.ops[eng]))
    ctx.close()
    return nc


def _host_prepare(inputs):
    f32 = np.float32
    bf = ml_dtypes.bfloat16
    wqa = np.asarray(inputs["L0_attn_a_w_qkv"], f32)
    woa = np.asarray(inputs["L0_attn_a_w_out"], f32)
    wqb = np.asarray(inputs["L0_attn_b_w_qkv"], f32).T
    wob = np.asarray(inputs["L0_attn_b_w_o"], f32).T
    gu = np.asarray(inputs["L0_ffn_expert_gate_up"], f32)
    dn = np.asarray(inputs["L0_ffn_expert_down"], f32)
    rg = np.asarray(inputs["L0_ffn_router_gate"], f32)
    qnw = np.asarray(inputs["L0_attn_b_q_norm_w"], f32)
    knw = np.asarray(inputs["L0_attn_b_k_norm_w"], f32)

    def qkv_blocks(w):
        cols = list(range(16, 32)) + list(range(32, 48)) + list(range(0, 16))
        return np.ascontiguousarray(
            np.stack([w[:, j * 128:(j + 1) * 128] for j in cols], axis=0))

    def of_blocks(w, n):
        return np.ascontiguousarray(
            np.stack([w[:, j * 128:(j + 1) * 128] for j in range(n)], axis=0))

    shared = dict(
        wqkv_a=qkv_blocks(wqa),
        wout_a=of_blocks(woa, 16),
        wqkv_b=qkv_blocks(wqb),
        wo_b=of_blocks(wob, 16),
        wgu=np.ascontiguousarray(
            np.stack([np.stack([np.stack([
                gu[e][:, 0::2][:, ic * 128:(ic + 1) * 128],
                gu[e][:, 1::2][:, ic * 128:(ic + 1) * 128]], 0)
                for ic in range(8)], 0) for e in range(8)], 0)),
        wdn=np.ascontiguousarray(
            np.stack([np.stack([dn[e][:, j * 128:(j + 1) * 128]
                                for j in range(16)], 0) for e in range(8)], 0)),
        wrt=np.ascontiguousarray(rg.T),
        qnw=np.ascontiguousarray(qnw.reshape(16, 128).T),
        knw=np.ascontiguousarray(knw.reshape(16, 128).T),
        ident=np.eye(128, dtype=f32),
    )

    x = np.asarray(inputs["L0_attn_a_x"], f32)
    inv_freq = 1.0 / (10000.0 ** (np.arange(0, 128, 2) / 128.0))

    per_core = []
    for c in range(8):
        b, p = c // 2, c % 2
        pos = np.arange(p * 1024, (p + 1) * 1024, dtype=np.float64)
        fr = pos[None, :] * inv_freq[:, None]
        ropec = np.concatenate([np.cos(fr), np.cos(fr)], 0).astype(f32)
        ropes = np.concatenate([np.sin(fr), np.sin(fr)], 0).astype(f32)
        qa = p * 1024 + np.arange(1024)
        ka = np.arange(2048)
        allow_a = (ka[:, None] <= qa[None, :])
        allow_b = allow_a & (qa[None, :] - ka[:, None] <= 1023)
        KILL = np.float32(-1e6)
        SHIFT_ = np.float32(-20.0)
        maskadd = np.zeros((2, 2, 8, 128, 512), f32)
        biasv = np.zeros((128, 64), f32)
        for li, allow in ((0, allow_a), (1, allow_b)):
            for qc in range(2):
                bset = list(range(4 * qc, 4 * qc + 4)) + \
                    list(range(4 * qc + 8, 4 * qc + 12))
                qsl = slice(qc * 512, qc * 512 + 512)
                for j, kt in enumerate(bset):
                    al = allow[kt * 128:(kt + 1) * 128, qsl]
                    maskadd[li, qc, j] = np.where(al, 0.0, KILL)
                for kt in range(16):
                    col = li * 32 + qc * 16 + kt
                    if kt in bset:
                        biasv[:, col] = SHIFT_
                    else:
                        al = allow[kt * 128:(kt + 1) * 128, qsl]
                        full = al.all()
                        none = not al.any()
                        assert full or none, (li, qc, kt, p)
                        biasv[:, col] = SHIFT_ if full else (SHIFT_ + KILL)
        d = dict(shared)
        d["x0"] = np.ascontiguousarray(x[b, p * 1024:(p + 1) * 1024, :].T)
        d["ropec"] = ropec
        d["ropes"] = ropes
        d["maskadd"] = maskadd.astype(bf)
        d["biasv"] = biasv
        per_core.append(d)
    return per_core


def kernel(**inputs):
    per_core = _host_prepare(inputs)
    nc = build_program()
    res = run_bass_kernel_spmd(nc, per_core, core_ids=list(range(8)))
    out = np.empty((B, S, D), np.float32)
    for c in range(8):
        b, p = c // 2, c % 2
        out[b, p * 1024:(p + 1) * 1024, :] = res.results[c]["y"].T
    return out


if __name__ == "__main__":
    nc = build_program()
    print("n_inst:", nc.n_instructions())
